# version 19
# baseline (speedup 1.0000x reference)
import contextlib
import numpy as np
import concourse.bass as bass
import concourse.mybir as mybir
from concourse.bass_utils import run_bass_kernel_spmd

F32 = mybir.dt.float32
BF16 = mybir.dt.bfloat16
I32 = mybir.dt.int32
AF = mybir.ActivationFunctionType
ALU = mybir.AluOpType

D = 1024
E = 2048
TSEQ = 2048
NSEQ = 4
TT = 512
TPS = TSEQ // TT
NJ = TT // 128
NKD = D // 128
NKE = E // 128
NTOK = NSEQ * TSEQ

ENGINES = ("sync", "tensor", "scalar", "vector", "gpsimd")


DMA_ROT = 8


class Emitter:
    def __init__(self):
        self.ops = {e: [] for e in ENGINES}
        self.counts = {}
        self.stream_idx = {}

    def sem(self, name):
        self.counts.setdefault(name, 0)

    def emit(self, eng, fn, waits=(), inc=None, inc_n=1, drain=False):
        val = None
        if inc is not None:
            self.counts[inc] = self.counts.get(inc, 0) + inc_n
            val = self.counts[inc]
        self.ops[eng].append((fn, tuple(waits), inc, inc_n, drain))
        return val

    def dma(self, eng, fn, stream, waits=()):
        idx = self.stream_idx.get(stream, 0)
        self.stream_idx[stream] = idx + 1
        sem = f"{stream}{idx % DMA_ROT}"
        self.sem(sem)
        w = list(waits)
        prev = 16 * (idx // DMA_ROT)
        if prev > 0:
            w.append((sem, prev))
        self.counts[sem] = self.counts.get(sem, 0) + 16
        self.ops[eng].append((fn, tuple(w), sem, 16, False))
        return (sem, 16 * (idx // DMA_ROT + 1))

    def stream_total(self, stream):
        n = self.stream_idx.get(stream, 0)
        toks = []
        for r in range(DMA_ROT):
            cnt = (n - r + DMA_ROT - 1) // DMA_ROT
            if cnt > 0:
                toks.append((f"{stream}{r}", 16 * cnt))
        return toks

    def c(self, name):
        return self.counts.get(name, 0)


def build_nc():
    nc = bass.Bass("TRN2", target_bir_lowering=False)

    def din(name, shape, dt=F32):
        return nc.dram_tensor(name, shape, dt, kind="ExternalInput").ap()

    x_h = din("x", [NTOK, D])
    w_in_h = din("in_wT", [D, 2 * E])
    w_dt_h = din("dt_wT", [E, E])
    w_out_h = din("out_wT", [E, D])
    w_ffn_h = din("ffn_wT", [D, D])
    w_bcd_h = din("bcd_wT", [E, 48])
    in_b_pm_h = din("in_b_pm", [128, 32])
    dt_b_pm_h = din("dt_b_pm", [128, 16])
    conv_b_pm_h = din("conv_b_pm", [128, 16])
    conv_w_pm_h = din("conv_w_pm", [128, 48])
    bcd_b3_h = din("bcd_b3", [16, 3])
    out_b_row_h = din("out_b_row", [1, D])
    ffn_b_row_h = din("ffn_b_row", [1, D])
    in_scale_h = din("in_scale", [1, 1])
    w_scale_h = din("w_scale", [1, 1])
    ident_f32_h = din("ident_f32", [128, 128])
    ident_bf_h = din("ident_bf", [128, 128], BF16)
    ones_row_f32_h = din("ones_row_f32", [1, 128])
    ones_row_bf_h = din("ones_row_bf", [1, 128], BF16)
    ones_col_bf_h = din("ones_col_bf", [128, 1], BF16)

    y_h = nc.dram_tensor("y", [NTOK, D], F32, kind="ExternalOutput").ap()

    sc_in = nc.dram_tensor("sc_in", [32 * 128 * 1024], BF16, kind="Internal").ap()
    sc_dt = nc.dram_tensor("sc_dt", [16 * 128 * 2048], BF16, kind="Internal").ap()
    sc_out = nc.dram_tensor("sc_out", [2 * 128 * 8192], BF16, kind="Internal").ap()

    em = Emitter()
    for s in ("pe", "act", "dve", "gp"):
        em.sem(s)

    es = contextlib.ExitStack()

    def sb(name, shape, dt=F32):
        return es.enter_context(nc.sbuf_tensor(name, shape, dt))

    def psumt(name, shape, dt=F32):
        return es.enter_context(nc.psum_tensor(name, shape, dt))

    xA = sb("xA", [128, 2 * D], BF16)
    xres = sb("xres", [128, 4 * D])
    x_fm = sb("x_fm", [128, NKD * (TT + 1)], BF16)
    xm = sb("xm", [128, NKE * (TT + 2)], BF16)
    xc = sb("xc", [128, NKE * TT], BF16)
    g_sb = sb("g_sb", [128, NKE * TT], BF16)
    cbuf = [[sb(f"cb{s}{t}", [128, TT], BF16) for t in range(2)] for s in range(2)]
    cvt = sb("cvt", [128, TT])
    sp_bf = [sb("sp0", [128, TT], BF16), sb("sp1", [128, TT], BF16)]
    w_in_sl = [sb(f"win{t}", [128, 2048], BF16) for t in range(4)]
    w_dt_sl = [sb(f"wdt{t}", [128, 2048], BF16) for t in range(6)]
    w_out_sl = [sb(f"wout{t}", [128, 4096], BF16) for t in range(2)]
    w_ffn_res = sb("wffn", [128, 8 * 1024], mybir.dt.float8e4)
    w_bcd_res = sb("wbcd", [128, 16 * 48], BF16)
    decay16 = sb("decay16", [16, TT])
    u16 = sb("u16", [16, TT])
    c16 = sb("c16", [16, TT])
    cs16 = sb("cs16", [16, TT])
    du16 = cs16
    st16s = sb("st16a", [16, TT])
    state_col = sb("state_col", [16, 1])
    ys_tm = sb("ys_tm", [128, NJ])
    r2f = sb("r2f", [128, NJ * D])
    ln1 = r2f
    scrF = sb("scrF", [128, 1024])
    qA = scrF[:, 0:1024]
    outb = [sb("outb0", [128, D]), sb("outb1", [128, D])]
    stat = sb("stat", [128, 16])
    xq8 = sb("xq8", [128, 2048], mybir.dt.float8e4)
    xqs = [xq8[:, 0:1024], xq8[:, 1024:2048]]
    in_b_pm = sb("c_in_b_pm", [128, 32])
    dt_b_pm = sb("c_dt_b_pm", [128, 16])
    conv_b_pm = sb("c_conv_b_pm", [128, 16])
    conv_w_pm = sb("c_conv_w_pm", [128, 48])
    bcd_b3 = sb("c_bcd_b3", [16, 3])
    out_b_row = sb("c_out_b_row", [1, D])
    ffn_b_row = sb("c_ffn_b_row", [1, D])
    outb_bc = sb("outb_bc", [128, D])
    bp_row_bf = sb("bp_row_bf", [1, D], BF16)
    sc_sc = sb("sc_sc", [1, 8])
    r_bc = sb("r_bc", [128, 1])
    sp_bc = sb("sp_bc", [128, 1])
    ident_f32 = sb("c_ident_f32", [128, 128])
    ident_bf = sb("c_ident_bf", [128, 128], BF16)
    ones_row_f32 = sb("c_ones_row_f32", [1, 128])
    ones_row_bf = sb("c_ones_row_bf", [1, 128], BF16)
    ones_col_bf = sb("c_ones_col_bf", [128, 1], BF16)

    xh_sb = out_b_row
    decay_t = ffn_b_row[0:1, 0:512]
    decay_row = ffn_b_row[0:1, 512:1024]

    P0 = psumt("P0", [128, 512])
    P1 = psumt("P1", [128, 512])
    P2 = psumt("P2", [128, 512])
    P3 = psumt("P3", [128, 512])
    P4 = psumt("P4", [128, 512])
    P5 = psumt("P5", [128, 512])
    PT = psumt("PT", [128, 512])
    PS1 = psumt("PS1", [128, 512])
    PMAIN = [P0, P1]
    PHALO = [P2, P3]

    bank_free = {}

    def bank_wait(bname):
        w = bank_free.pop(bname, None)
        return [w] if w is not None else []

    def bank_set(bname, sem, cnt):
        bank_free[bname] = (sem, cnt)

    const_loads = [
        (in_b_pm, in_b_pm_h), (dt_b_pm, dt_b_pm_h), (conv_b_pm, conv_b_pm_h),
        (conv_w_pm, conv_w_pm_h), (bcd_b3, bcd_b3_h), (out_b_row, out_b_row_h),
        (ffn_b_row, ffn_b_row_h), (ident_f32, ident_f32_h),
        (ident_bf, ident_bf_h),
        (ones_row_f32, ones_row_f32_h), (ones_row_bf, ones_row_bf_h),
        (ones_col_bf, ones_col_bf_h),
    ]
    cl_tok = None
    for dst, srct in const_loads:
        cl_tok = em.dma("sync", (lambda d=dst, s=srct: nc.sync.dma_start(out=d[:, :], in_=s[:, :])),
                        stream="cl")
    cl_tok = em.dma("sync", lambda: nc.sync.dma_start(out=sc_sc[0:1, 0:1], in_=in_scale_h[:, :]), stream="cl")
    cl_tok = em.dma("sync", lambda: nc.sync.dma_start(out=sc_sc[0:1, 1:2], in_=w_scale_h[:, :]), stream="cl")
    cl_all = em.stream_total("cl")

    em.emit("vector", lambda: nc.vector.tensor_tensor(
        sc_sc[0:1, 2:3], sc_sc[0:1, 0:1], sc_sc[0:1, 1:2], ALU.mult),
        waits=cl_all, drain=True)
    em.emit("vector", lambda: nc.vector.reciprocal(sc_sc[0:1, 3:4], sc_sc[0:1, 2:3]), drain=True)
    em.emit("vector", lambda: nc.vector.reciprocal(sc_sc[0:1, 4:5], sc_sc[0:1, 0:1]), drain=True)
    em.emit("vector", lambda: nc.vector.tensor_scalar(
        out=bp_row_bf[:, :], in0=ffn_b_row[:, :], scalar1=sc_sc[0:1, 3:4],
        scalar2=None, op0=ALU.mult), drain=True)
    dve_prep = em.emit("vector", lambda: nc.vector.tensor_copy(
        sc_sc[0:1, 5:6], sc_sc[0:1, 0:1]), drain=True, inc="dve")
    em.emit("tensor", lambda: nc.tensor.matmul(
        PT[:, 0:1], ones_row_f32[0:1, :], sc_sc[0:1, 4:5], start=True, stop=True),
        waits=[("dve", dve_prep)])
    pe_bc = em.emit("tensor", lambda: nc.tensor.matmul(
        PT[:, 1:2], ones_row_f32[0:1, :], sc_sc[0:1, 2:3], start=True, stop=True),
        inc="pe")
    em.emit("vector", lambda: nc.vector.tensor_copy(r_bc[:, :], PT[:, 0:1]),
            waits=[("pe", pe_bc)], drain=True)
    dve_bc = em.emit("vector", lambda: nc.vector.tensor_copy(sp_bc[:, :], PT[:, 1:2]),
                     drain=True, inc="dve")
    pe_ob = em.emit("tensor", lambda: nc.tensor.matmul(
        PS1[:, 0:512], ones_row_f32[0:1, :], out_b_row[0:1, 0:512],
        start=True, stop=True), waits=[("dve", dve_prep)], inc="pe")
    em.emit("vector", lambda: nc.vector.tensor_copy(outb_bc[:, 0:512], PS1[:, 0:512]),
            waits=[("pe", pe_ob)], drain=True, inc="dve")
    dve_ob1 = em.c("dve")
    pe_ob2 = em.emit("tensor", lambda: nc.tensor.matmul(
        PS1[:, 0:512], ones_row_f32[0:1, :], out_b_row[0:1, 512:1024],
        start=True, stop=True), waits=[("dve", dve_ob1)], inc="pe")
    dve_ob2 = em.emit("vector", lambda: nc.vector.tensor_copy(
        outb_bc[:, 512:1024], PS1[:, 0:512]),
        waits=[("pe", pe_ob2)], drain=True, inc="dve")
    bank_set("PT", "dve", dve_bc)
    bank_set("PS1", "dve", dve_ob2)
    gate_outb_row = ("pe", pe_ob2)
    prep_outb_done = ("dve", dve_ob2)

    X0_TOKS = {}
    for jx in range(2):
        def mk_x0(jx=jx):
            srcp = bass.AP(x_h.tensor, (jx * 128) * D, [[D, 128], [1, D]])
            dst = bass.AP(xA, (jx % 2) * D, [[2 * D, 128], [1, D]])
            return nc.gpsimd.dma_start(out=dst, in_=srcp)
        X0_TOKS[jx] = em.dma("gpsimd", mk_x0, stream="x")

    toks_in = []
    for k in range(NKD):
        for ch in range(2):
            t = em.dma("gpsimd", (lambda k=k, ch=ch: nc.gpsimd.dma_start(
                out=bass.AP(sc_in.tensor, k * 128 + ch * 1024,
                            [[2048, 128], [128 * 2048, 16], [1, 128]]),
                in_=bass.AP(w_in_h.tensor, k * 128 * 4096 + ch * 128,
                            [[4096, 128], [256, 16], [1, 128]]))),
                stream="pr")
            toks_in.append(t)
    toks_dt = []
    for k in range(NKE):
        t = em.dma("gpsimd", (lambda k=k: nc.gpsimd.dma_start(
            out=bass.AP(sc_dt.tensor, k * 128, [[2048, 128], [128 * 2048, 16], [1, 128]]),
            in_=bass.AP(w_dt_h.tensor, k * 128 * 2048, [[2048, 128], [128, 16], [1, 128]]))),
            stream="pr")
        toks_dt.append(t)
    toks_out = []
    for k in range(NKE):
        t = em.dma("gpsimd", (lambda k=k: nc.gpsimd.dma_start(
            out=bass.AP(sc_out.tensor, k * 512, [[8192, 128], [128 * 8192, 2], [1, 512]]),
            in_=bass.AP(w_out_h.tensor, k * 128 * 1024, [[1024, 128], [512, 2], [1, 512]]))),
            stream="pr")
        toks_out.append(t)
    toks_ffn = []
    ffn_stage_done = [None, None]
    for k in range(NKD):
        stg = w_out_sl[k % 2]
        wl = []
        if ffn_stage_done[k % 2] is not None:
            wl.append(("dve", ffn_stage_done[k % 2]))
        t = em.dma("gpsimd", (lambda k=k, stg=stg: nc.gpsimd.dma_start(
            out=stg[:, 0:1024],
            in_=w_ffn_h[k * 128:(k + 1) * 128, :])),
            stream="pr", waits=wl)
        dve_c = em.emit("vector", (lambda k=k, stg=stg: nc.vector.tensor_copy(
            w_ffn_res[:, k * 1024:(k + 1) * 1024], stg[:, 0:1024])),
            waits=[t], drain=True, inc="dve")
        ffn_stage_done[k % 2] = dve_c
        toks_ffn.append(("dve", dve_c))
    ffn_stage_toks = [("dve", ffn_stage_done[0]), ("dve", ffn_stage_done[1])]
    toks_bcd = []
    for k in range(NKE):
        t = em.dma("gpsimd", (lambda k=k: nc.gpsimd.dma_start(
            out=w_bcd_res[:, k * 48:(k + 1) * 48],
            in_=w_bcd_h[k * 128:(k + 1) * 128, :])),
            stream="pr")
        toks_bcd.append(t)
    toks_in = toks_in[-DMA_ROT:]
    toks_dt = toks_dt[-DMA_ROT:]
    toks_out = toks_out[-DMA_ROT:]
    toks_ffn = toks_ffn[-DMA_ROT:]
    toks_bcd = toks_bcd[-DMA_ROT:]

    G = {
        "x_slot": [None, None],
        "xres": [None] * 4,
        "xfm": None,
        "inslab": [None] * 4,
        "dtslab": [None] * 6,
        "slab_ring": [0],
        "outslab": ffn_stage_toks[:],
        "sp": [None, None],
        "cbuf": [None, None],
        "outb": [None, None],
        "xq": [None, None],
        "r2f": None,
        "ln1": None,
        "g": None,
        "xc": None,
        "xm_taps": None,
        "cs": None,
        "scan": None,
    }
    first_use = {"in": True, "dt": True, "out": True, "ffn": True, "bcd": True}
    TS_ = {}

    def tile_base(i):
        s, ti = divmod(i, TPS)
        return s * TSEQ + ti * TT, (ti == 0), (ti == TPS - 1)

    def emit_front(i):
        base, first, last = tile_base(i)
        st = TS_.setdefault(i, {})
        x_toks = {}

        def load_x(jx):
            if i == 0 and jx < 2:
                x_toks[jx] = X0_TOKS[jx]
                return
            slot = (i * NJ + jx) % 2
            wl = [G["x_slot"][slot]] if G["x_slot"][slot] is not None else []
            def mk_x(base=base, jx=jx, slot=slot):
                srcp = bass.AP(x_h.tensor, (base + jx * 128) * D, [[D, 128], [1, D]])
                dst = bass.AP(xA, slot * D, [[2 * D, 128], [1, D]])
                return nc.gpsimd.dma_start(out=dst, in_=srcp)
            x_toks[jx] = em.dma("gpsimd", mk_x, stream="x", waits=wl)
        load_x(0)
        load_x(1)
        if not last:
            def mk_xh(base=base):
                return nc.sync.dma_start(
                    out=xh_sb[0:1, :], in_=x_h[base + TT:base + TT + 1, :])
            st["xh_tok"] = em.dma("sync", mk_xh, stream="xh",
                                  waits=([gate_outb_row] if i == 0 else []))
        BANKS4 = [P0, P1, P2, P3]
        for jj in range(NJ):
            slot = (i * NJ + jj) % 2
            last_pe = None
            for c4 in range(2):
                bidx = (jj * 2 + c4) % 4
                bank, bname = BANKS4[bidx], f"P{bidx}"
                wl = [x_toks[jj]] + bank_wait(bname)
                pe_v = None
                for q in range(4):
                    c = c4 * 4 + q
                    pe_v = em.emit("tensor", (lambda c=c, q=q, bank=bank, slot=slot:
                        nc.tensor.transpose(
                            bank[:, q * 64:(q + 1) * 64].bitcast(BF16),
                            bass.AP(xA, slot * D + c * 128, [[2 * D, 128], [1, 128]]),
                            ident_bf[:, :])),
                        waits=(wl if q == 0 else []), inc=("pe" if q == 3 else None))
                awl = [("pe", pe_v)]
                if jj == 0 and c4 == 0 and G["xfm"] is not None:
                    awl.append(G["xfm"])
                act_v = em.emit("scalar", (lambda jj=jj, c4=c4, bank=bank:
                    nc.scalar.copy(
                        bass.AP(x_fm, c4 * 4 * (TT + 1) + jj * 128,
                                [[NKD * (TT + 1), 128], [TT + 1, 4], [1, 128]]),
                        bank[:, 0:256].bitcast(BF16))),
                    waits=awl, drain=True, inc="act")
                bank_set(bname, "act", act_v)
                last_pe = pe_v
            G["x_slot"][slot] = ("pe", last_pe)
            if jj + 2 < NJ:
                load_x(jj + 2)
        if not last:
            for c4 in range(2):
                bank, bname = BANKS4[c4], f"P{c4}"
                wl = [st["xh_tok"]] + bank_wait(bname)
                pe_v = None
                for q in range(4):
                    c = c4 * 4 + q
                    pe_v = em.emit("tensor", (lambda c=c, q=q, bank=bank:
                        nc.tensor.transpose(
                            bank[:, q:q + 1],
                            xh_sb[0:1, c * 128:(c + 1) * 128],
                            ident_f32[0:1, 0:1])),
                        waits=(wl if q == 0 else []), inc=("pe" if q == 3 else None))
                act_v = em.emit("scalar", (lambda c4=c4, bank=bank:
                    nc.scalar.copy(
                        bass.AP(x_fm, c4 * 4 * (TT + 1) + TT,
                                [[NKD * (TT + 1), 128], [TT + 1, 4], [1, 1]]),
                        bank[:, 0:4])),
                    waits=[("pe", pe_v)], drain=True, inc="act")
                bank_set(bname, "act", act_v)
        st["xfm_ready"] = em.c("act")

        wl = [G["xm_taps"]] if G["xm_taps"] is not None else []
        if first:
            def mk_hl():
                return nc.vector.memset(
                    bass.AP(xm, 0, [[NKE * (TT + 2), 128], [TT + 2, NKE], [1, 1]]), 0.0)
        else:
            def mk_hl():
                srcp = bass.AP(xm, TT, [[NKE * (TT + 2), 128], [TT + 2, NKE], [1, 1]])
                dst = bass.AP(xm, 0, [[NKE * (TT + 2), 128], [TT + 2, NKE], [1, 1]])
                return nc.vector.tensor_copy(dst, srcp)
        dve_hl = em.emit("vector", mk_hl, waits=wl, drain=True, inc="dve")
        if last:
            dve_hl = em.emit("vector", lambda: nc.vector.memset(
                bass.AP(xm, TT + 1, [[NKE * (TT + 2), 128], [TT + 2, NKE], [1, 1]]), 0.0),
                drain=True, inc="dve")
        st["dve_hl"] = dve_hl

        st["inproj_act"] = {}
        for mp in range(8):
            b = mp % 4
            wl = [G["inslab"][b]] if G["inslab"][b] is not None else []
            if first_use["in"]:
                wl.extend(toks_in)
                first_use["in"] = False
            def mk_sl(mp=mp, b=b):
                srcp = bass.AP(sc_in.tensor, mp * 128 * 2048, [[2048, 128], [1, 2048]])
                dstp = bass.AP(w_in_sl[b], 0, [[2048, 128], [1, 2048]])
                return nc.sync.dma_start(out=dstp, in_=srcp)
            sl_v = em.dma("sync", mk_sl, stream="wi", waits=wl)
            pe_v = None
            for half in range(2):
                m = 2 * mp + half
                bname = "P0" if m % 2 == 0 else "P1"
                wl = [sl_v, ("act", st["xfm_ready"])] + bank_wait(bname)
                for k in range(NKD):
                    pe_v = em.emit("tensor", (lambda k=k, b=b, half=half, m=m:
                        nc.tensor.matmul(
                            PMAIN[m % 2][:, 0:512],
                            w_in_sl[b][:, half * 1024 + k * 128: half * 1024 + (k + 1) * 128],
                            x_fm[:, k * (TT + 1): k * (TT + 1) + TT],
                            start=(k == 0), stop=(k == NKD - 1))),
                        waits=(wl if k == 0 else []), inc=("pe" if k == NKD - 1 else None))
                act_v = em.emit("scalar", (lambda m=m:
                    nc.scalar.activation(
                        out=xm[:, m * (TT + 2) + 1: m * (TT + 2) + 1 + TT],
                        in_=PMAIN[m % 2][:, 0:512], func=AF.Identity,
                        bias=in_b_pm[:, m:m + 1], scale=1.0)),
                    waits=[("pe", pe_v), ("dve", dve_hl)], drain=True, inc="act")
                bank_set(bname, "act", act_v)
                st["inproj_act"][m] = act_v
                if not last:
                    hname = "P2" if m % 2 == 0 else "P3"
                    hbank = PHALO[m % 2]
                    hwl = bank_wait(hname)
                    for k in range(NKD):
                        pe_v = em.emit("tensor", (lambda k=k, b=b, half=half, hbank=hbank:
                            nc.tensor.matmul(
                                hbank[:, 0:1],
                                w_in_sl[b][:, half * 1024 + k * 128: half * 1024 + (k + 1) * 128],
                                x_fm[:, k * (TT + 1) + TT: k * (TT + 1) + TT + 1],
                                start=(k == 0), stop=(k == NKD - 1))),
                            waits=(hwl if k == 0 else []),
                            inc=("pe" if k == NKD - 1 else None))
                    act_v2 = em.emit("scalar", (lambda m=m, hbank=hbank:
                        nc.scalar.activation(
                            out=xm[:, m * (TT + 2) + TT + 1: m * (TT + 2) + TT + 2],
                            in_=hbank[:, 0:1], func=AF.Identity,
                            bias=in_b_pm[:, m:m + 1], scale=1.0)),
                        waits=[("pe", pe_v)], drain=True, inc="act")
                    bank_set(hname, "act", act_v2)
                    st["inproj_act"][m] = act_v2
            G["inslab"][b] = ("pe", pe_v)

    def ring_next():
        r = G["slab_ring"][0]
        G["slab_ring"][0] = (r + 1) % 6
        return r

    def issue_gate_pair(i, g):
        st = TS_[i]
        m0 = 16 + 2 * g
        b = ring_next()
        wl = [G["dtslab"][b]] if G["dtslab"][b] is not None else []
        def mk_sl(g=g, b=b):
            srcp = bass.AP(sc_in.tensor, (8 + g) * 128 * 2048,
                           [[2048, 128], [1, 2048]])
            dstp = bass.AP(w_dt_sl[b], 0, [[2048, 128], [1, 2048]])
            return nc.sync.dma_start(out=dstp, in_=srcp)
        st.setdefault("gate_sl", {})[g] = (
            em.dma("sync", mk_sl, stream="wg", waits=wl), b)
        G["dtslab"][b] = None

    def emit_gate_part(i, part):
        base, first, last = tile_base(i)
        st = TS_[i]
        for pair in range(2):
            g = 2 * part + pair
            sl_v, b = st["gate_sl"][g]
            pe_v = None
            for half in range(2):
                mg = 4 * part + 2 * pair + half
                gbank = [P0, P1, P2, P3][mg % 4]
                bname = f"P{mg % 4}"
                wl = [sl_v, ("act", st["xfm_ready"])] + bank_wait(bname)
                for k in range(NKD):
                    pe_v = em.emit("tensor", (lambda k=k, b=b, half=half, gbank=gbank:
                        nc.tensor.matmul(
                            gbank[:, 0:512],
                            w_dt_sl[b][:, half * 1024 + k * 128: half * 1024 + (k + 1) * 128],
                            x_fm[:, k * (TT + 1): k * (TT + 1) + TT],
                            start=(k == 0), stop=(k == NKD - 1))),
                        waits=(wl if k == 0 else []), inc=("pe" if k == NKD - 1 else None))
                gwl = [("pe", pe_v)]
                if mg == 0 and G["g"] is not None:
                    gwl.append(G["g"])
                act_v = em.emit("scalar", (lambda mg=mg, gbank=gbank:
                    nc.scalar.activation(
                        out=g_sb[:, mg * TT:(mg + 1) * TT],
                        in_=gbank[:, 0:512], func=AF.Silu,
                        bias=in_b_pm[:, 16 + mg:16 + mg + 1], scale=1.0)),
                    waits=gwl, drain=True, inc="act")
                bank_set(bname, "act", act_v)
            G["dtslab"][b] = ("pe", pe_v)
            if g + 6 < 8:
                issue_gate_pair(i, g + 6)
        if part == 3:
            G["xfm"] = ("pe", em.c("pe"))

    def emit_conv_part(i, part):
        st = TS_[i]
        if part == 0:
            st.setdefault("conv_act", {})
        for m in range(4 * part, 4 * part + 4):
            cb = cbuf[m % 2]
            base_c = m * (TT + 2)
            wl0 = [("act", st["inproj_act"][m])]
            if G["cbuf"][m % 2] is not None:
                wl0.append(G["cbuf"][m % 2])
            em.emit("vector", (lambda m=m, cb=cb, base_c=base_c:
                nc.vector.tensor_scalar(
                    out=cb[0][:, :], in0=xm[:, base_c: base_c + TT],
                    scalar1=conv_w_pm[:, 3 * m: 3 * m + 1], scalar2=None,
                    op0=ALU.mult)),
                waits=wl0, drain=True)
            em.emit("vector", (lambda m=m, cb=cb, base_c=base_c:
                nc.vector.scalar_tensor_tensor(
                    out=cb[0][:, :], in0=xm[:, base_c + 1: base_c + 1 + TT],
                    scalar=conv_w_pm[:, 3 * m + 1: 3 * m + 2], in1=cb[0][:, :],
                    op0=ALU.mult, op1=ALU.add)),
                drain=True)
            dve_v = em.emit("vector", (lambda m=m, cb=cb, base_c=base_c:
                nc.vector.scalar_tensor_tensor(
                    out=cb[0][:, :], in0=xm[:, base_c + 2: base_c + 2 + TT],
                    scalar=conv_w_pm[:, 3 * m + 2: 3 * m + 3], in1=cb[0][:, :],
                    op0=ALU.mult, op1=ALU.add)),
                drain=True, inc="dve")
            xwl = [("dve", dve_v)]
            if m == 0 and G["xc"] is not None:
                xwl.append(G["xc"])
            act_v = em.emit("scalar", (lambda m=m, cb=cb:
                nc.scalar.activation(
                    out=xc[:, m * TT:(m + 1) * TT], in_=cb[0][:, :], func=AF.Silu,
                    bias=conv_b_pm[:, m:m + 1], scale=1.0)),
                waits=xwl, drain=True, inc="act")
            st["conv_act"][m] = act_v
            G["cbuf"][m % 2] = ("act", act_v)
        if part == 3:
            st["xc_ready"] = em.c("act")
            G["xm_taps"] = ("dve", em.c("dve"))

    def emit_mixer(i):
        st = TS_[i]
        base, first, last = tile_base(i)
        xc_ready = st["xc_ready"]
        for jjx in range(NJ):
            prep_xres(i, jjx)
        pend_ones = None
        pe_v2 = None
        for m in range(NKE):
            b = m % 2
            sbi = ring_next()
            wl = [G["dtslab"][sbi]] if G["dtslab"][sbi] is not None else []
            if first_use["dt"]:
                wl.extend(toks_dt)
                first_use["dt"] = False
            def mk_sl(m=m, sbi=sbi):
                srcp = bass.AP(sc_dt.tensor, m * 128 * 2048, [[2048, 128], [1, 2048]])
                return nc.sync.dma_start(out=w_dt_sl[sbi][:, :], in_=srcp)
            sl_v = em.dma("sync", mk_sl, stream="wd", waits=wl)
            bname = "P0" if b == 0 else "P1"
            wl = [sl_v, ("act", xc_ready)] + bank_wait(bname)
            pe_v = None
            for k in range(NKE):
                pe_v = em.emit("tensor", (lambda k=k, b=b, sbi=sbi:
                    nc.tensor.matmul(
                        PMAIN[b][:, 0:512],
                        w_dt_sl[sbi][:, k * 128:(k + 1) * 128],
                        xc[:, k * TT:(k + 1) * TT],
                        start=(k == 0), stop=(k == NKE - 1))),
                    waits=(wl if k == 0 else []), inc=("pe" if k == NKE - 1 else None))
            G["dtslab"][sbi] = ("pe", pe_v)
            if pend_ones is not None:
                pe_v2 = em.emit("tensor", pend_ones[0], waits=pend_ones[1], inc="pe")
                G["sp"][pend_ones[2]] = ("pe", pe_v2)
            swl = [("pe", pe_v)]
            if G["sp"][b] is not None:
                swl.append(G["sp"][b])
            em.emit("scalar", (lambda b=b, m=m:
                nc.scalar.activation(
                    out=cvt[:, :], in_=PMAIN[b][:, 0:512], func=AF.Exp,
                    bias=dt_b_pm[:, m:m + 1], scale=1.0)),
                waits=swl, drain=True)
            act_v = em.emit("scalar", (lambda b=b:
                nc.scalar.activation(
                    out=sp_bf[b][:, :], in_=cvt[:, :], func=AF.Ln,
                    bias=1.0, scale=1.0)),
                drain=True, inc="act")
            bank_set(bname, "act", act_v)
            owl = [("act", act_v)]
            if m == 0:
                owl += bank_wait("PS1")
            pend_ones = ((lambda b=b, m=m: nc.tensor.matmul(
                PS1[0:1, 0:512], ones_col_bf[:, 0:1], sp_bf[b][:, :],
                start=(m == 0), stop=(m == NKE - 1))), owl, b)
        pe_v2 = em.emit("tensor", pend_ones[0], waits=pend_ones[1], inc="pe")
        G["sp"][pend_ones[2]] = ("pe", pe_v2)
        dtsum_done = pe_v2

        dve_v = em.emit("vector", lambda: nc.vector.tensor_scalar(
            out=decay_t, in0=PS1[0:1, 0:512],
            scalar1=float(-1.0 / 2048.0), scalar2=-10.0,
            op0=ALU.mult, op1=ALU.max),
            waits=[("pe", dtsum_done)], drain=True, inc="dve")
        bank_set("PS1", "dve", dve_v)
        act_v = em.emit("scalar", lambda: nc.scalar.activation(
            out=decay_row, in_=decay_t, func=AF.Exp),
            waits=[("dve", dve_v)], drain=True, inc="act")
        wl = [("act", act_v)] + bank_wait("PT")
        pe_v = em.emit("tensor", lambda: nc.tensor.matmul(
            PT[:, 0:512], ones_row_f32[0:1, :], decay_row,
            start=True, stop=True),
            waits=wl, inc="pe")
        dve_v = em.emit("vector", lambda: nc.vector.tensor_copy(
            decay16[:, :], PT[0:16, 0:512]),
            waits=[("pe", pe_v)], drain=True, inc="dve")
        bank_set("PT", "dve", dve_v)

        bcd_banks = [("P2", P2, 0), ("P3", P3, 16), ("P4", P4, 32)]
        bcd_done = {}
        for bn, bank, off in bcd_banks:
            wl = [("act", xc_ready)] + bank_wait(bn)
            if first_use["bcd"]:
                wl.extend(toks_bcd)
                first_use["bcd"] = False
            pe_v = None
            for k in range(NKE):
                pe_v = em.emit("tensor", (lambda k=k, bank=bank, off=off:
                    nc.tensor.matmul(
                        bank[0:16, 0:512],
                        w_bcd_res[:, k * 48 + off: k * 48 + off + 16],
                        xc[:, k * TT:(k + 1) * TT],
                        start=(k == 0), stop=(k == NKE - 1))),
                    waits=(wl if k == 0 else []), inc=("pe" if k == NKE - 1 else None))
            bcd_done[bn] = pe_v

        em.emit("vector", lambda: nc.vector.tensor_scalar(
            out=u16[:, :], in0=P2[0:16, 0:512], scalar1=bcd_b3[:, 0:1],
            scalar2=None, op0=ALU.add),
            waits=[("pe", bcd_done["P2"])], drain=True)
        em.emit("vector", lambda: nc.vector.tensor_scalar(
            out=du16[:, :], in0=P4[0:16, 0:512], scalar1=bcd_b3[:, 2:3],
            scalar2=None, op0=ALU.add),
            waits=[("pe", bcd_done["P4"])] + ([G["cs"]] if G["cs"] is not None else []),
            drain=True)
        em.emit("vector", lambda: nc.vector.tensor_tensor(
            u16[:, :], u16[:, :], du16[:, :], ALU.mult), drain=True)
        dve_c16 = em.emit("vector", lambda: nc.vector.tensor_scalar(
            out=c16[:, :], in0=P3[0:16, 0:512], scalar1=bcd_b3[:, 1:2],
            scalar2=None, op0=ALU.add),
            waits=[("pe", bcd_done["P3"])], drain=True, inc="dve")
        bank_set("P2", "dve", dve_c16)
        bank_set("P3", "dve", dve_c16)
        bank_set("P4", "dve", dve_c16)

        if first:
            em.emit("vector", lambda: nc.vector.tensor_tensor_scan(
                st16s[:, :], decay16[:, :], u16[:, :], 0.0,
                ALU.mult, ALU.add), drain=True)
        else:
            em.emit("vector", lambda: nc.vector.tensor_tensor_scan(
                st16s[:, :], decay16[:, :], u16[:, :],
                state_col[:, 0:1], ALU.mult, ALU.add), drain=True)
        em.emit("vector", lambda: nc.vector.tensor_copy(
            state_col[:, 0:1], st16s[:, TT - 1:TT]), drain=True)
        cs_ready = em.emit("vector", lambda: nc.vector.tensor_tensor(
            cs16[:, :], c16[:, :], st16s[:, :], ALU.mult),
            drain=True, inc="dve")
        st["cs_ready"] = cs_ready

    def emit_ys(i):
        st = TS_[i]
        cs_ready = st["cs_ready"]
        ys_ready = {}
        last_cs_pe = None
        for jj in range(NJ):
            b = jj % 2
            bank, bname = PMAIN[b], ("P0" if b == 0 else "P1")
            wl = [("dve", cs_ready)] + bank_wait(bname)
            pe_v = em.emit("tensor", (lambda jj=jj, bank=bank:
                nc.tensor.transpose(
                    bank[:, 0:16], cs16[0:16, jj * 128:(jj + 1) * 128],
                    ident_f32[0:16, 0:16])),
                waits=wl, inc="pe")
            dve_v = em.emit("vector", (lambda jj=jj, bank=bank:
                nc.vector.tensor_reduce(
                    out=ys_tm[:, jj:jj + 1], in_=bank[:, 0:16],
                    axis=mybir.AxisListType.X, op=ALU.add)),
                waits=[("pe", pe_v)], drain=True, inc="dve")
            ys_ready[jj] = dve_v
            bank_set(bname, "dve", dve_v)
            last_cs_pe = pe_v
        G["cs"] = ("pe", last_cs_pe)
        st["ys_ready"] = ys_ready

    def emit_out(i):
        st = TS_[i]
        base, first, last = tile_base(i)
        ys_ready = st["ys_ready"]
        evac_out = {}
        pe_last = {}
        PBANKS4 = [("P0", P0), ("P1", P1), ("P2", P2), ("P3", P3)]
        for n2 in range(2):
            for h in range(2):
                ob_i = (n2 * 2 + h) % 2
                wl = [G["outslab"][ob_i]] if G["outslab"][ob_i] is not None else []
                if first_use["out"]:
                    wl.extend(toks_out)
                    first_use["out"] = False
                def mk_sl(n2=n2, h=h, ob_i=ob_i):
                    srcp = bass.AP(sc_out.tensor, n2 * 128 * 8192 + h * 4096,
                                   [[8192, 128], [1, 4096]])
                    return nc.scalar.dma_start(out=w_out_sl[ob_i][:, :], in_=srcp)
                sl_v = em.dma("scalar", mk_sl, stream="wo", waits=wl)
                for jj in range(NJ):
                    bname, bank = PBANKS4[jj]
                    wl = [sl_v]
                    if h == 0:
                        wl += bank_wait(bname)
                    for k8 in range(8):
                        k = h * 8 + k8
                        v = em.emit("tensor", (lambda k=k, k8=k8, jj=jj, bank=bank, h=h, ob_i=ob_i:
                            nc.tensor.matmul(
                                bank[:, 0:512],
                                g_sb[:, k * TT + jj * 128: k * TT + (jj + 1) * 128],
                                w_out_sl[ob_i][:, k8 * 512:(k8 + 1) * 512],
                                start=(k == 0), stop=(h == 1 and k8 == 7))),
                            waits=(wl if k8 == 0 else []),
                            inc=("pe" if k8 == 7 else None))
                        if k8 == 7:
                            pe_last[jj] = v
                G["outslab"][ob_i] = ("pe", em.c("pe"))
                if h == 1:
                    for jj in range(NJ):
                        bname, bank = PBANKS4[jj]
                        awl = [("pe", pe_last[jj])]
                        if n2 == 0 and jj == 0 and G["r2f"] is not None:
                            awl.append(G["r2f"])
                        dve_v = em.emit("vector", (lambda bank=bank, jj=jj, n2=n2:
                            nc.vector.scalar_tensor_tensor(
                                out=r2f[:, jj * D + n2 * 512: jj * D + n2 * 512 + 512],
                                in0=bank[:, 0:512],
                                scalar=ys_tm[:, jj:jj + 1],
                                in1=bass.AP(xres, jj * D + n2 * 512,
                                            [[4 * D, 128], [1, 512]]),
                                op0=ALU.mult, op1=ALU.add)),
                            waits=awl, drain=True, inc="dve")
                        bank_set(bname, "dve", dve_v)
                        evac_out[(n2, jj)] = dve_v
        G["g"] = ("pe", em.c("pe"))
        G["xc"] = ("pe", em.c("pe"))

        res_done = {}
        for jj in range(NJ):
            res_done[jj] = evac_out[(1, jj)]
            G["xres"][jj] = ("dve", evac_out[(1, jj)])
        st["res_done"] = res_done

    def prep_xres(i, jj):
        base, first, last = tile_base(i)
        wl = [G["xres"][jj]] if G["xres"][jj] is not None else []
        def mk_xr(base=base, jj=jj):
            srcp = bass.AP(x_h.tensor, (base + jj * 128) * D, [[D, 128], [1, D]])
            dst = bass.AP(xres, jj * D, [[4 * D, 128], [1, D]])
            return nc.sync.dma_start(out=dst, in_=srcp)
        xr_tok = em.dma("sync", mk_xr, stream="xr", waits=wl)
        wl2 = [xr_tok]
        if i == 0 and jj == 0:
            wl2.append(prep_outb_done)
        em.emit("vector", (lambda jj=jj: nc.vector.scalar_tensor_tensor(
            out=bass.AP(xres, jj * D, [[4 * D, 128], [1, D]]),
            in0=bass.AP(xres, jj * D, [[4 * D, 128], [1, D]]),
            scalar=2.0, in1=outb_bc[:, :],
            op0=ALU.mult, op1=ALU.add)),
            waits=wl2, drain=True, inc="dve")

    def emit_ln(src_fn, waits):
        em.emit("vector", lambda src_fn=src_fn: nc.vector.bn_stats(
            stat[:, 0:6], src_fn()[:, 0:512]), waits=waits, drain=True)
        em.emit("vector", lambda src_fn=src_fn: nc.vector.bn_stats(
            stat[:, 6:12], src_fn()[:, 512:1024]), drain=True)
        em.emit("vector", lambda: nc.vector.bn_aggr(
            stat[:, 12:14], stat[:, 0:12]), drain=True)
        em.emit("vector", lambda: nc.vector.tensor_scalar(
            out=stat[:, 14:15], in0=stat[:, 12:13], scalar1=-1.0,
            scalar2=None, op0=ALU.mult), drain=True)
        em.emit("vector", lambda: nc.vector.tensor_scalar(
            out=stat[:, 2:3], in0=stat[:, 13:14], scalar1=1e-5, scalar2=None,
            op0=ALU.add), drain=True)
        dve_v = em.emit("vector", lambda: nc.vector.reciprocal(
            stat[:, 3:4], stat[:, 2:3]), drain=True, inc="dve")
        act_v = em.emit("scalar", lambda: nc.scalar.activation(
            out=stat[:, 6:7], in_=stat[:, 3:4], func=AF.Sqrt),
            waits=[("dve", dve_v)], drain=True, inc="act")
        return act_v

    def emit_tail_preA(i, jj):
        st = TS_[i]
        res_done = st["res_done"]
        wl = [("dve", res_done[jj])]
        if G["ln1"] is not None and jj == 0:
            wl.append(G["ln1"])
        ln_act = emit_ln(lambda jj=jj: r2f[:, jj * D:(jj + 1) * D], wl)
        ln1_done = em.emit("vector", (lambda jj=jj: nc.vector.tensor_scalar(
            out=ln1[:, jj * D:(jj + 1) * D], in0=r2f[:, jj * D:(jj + 1) * D],
            scalar1=stat[:, 14:15], scalar2=stat[:, 6:7],
            op0=ALU.add, op1=ALU.mult)),
            waits=[("act", ln_act)], drain=True, inc="dve")
        st.setdefault("ln1_done", {})[jj] = ln1_done
        if jj == NJ - 1:
            G["r2f"] = ("dve", ln1_done)

    def emit_tail_preB(i, jj):
        st = TS_[i]
        base, first, last = tile_base(i)
        ln1_done = st["ln1_done"][jj]
        if True:
            for half in range(2):
                bank, bname = (PT, "PT") if half == 0 else (PS1, "PS1")
                wl0 = [("dve", ln1_done)] + bank_wait(bname)
                pe_v = None
                for c4 in range(4):
                    c = half * 4 + c4
                    pe_v = em.emit("tensor", (lambda jj=jj, c=c, c4=c4, bank=bank:
                        nc.tensor.transpose(
                            bank[:, c4 * 128:(c4 + 1) * 128],
                            ln1[:, jj * D + c * 128: jj * D + (c + 1) * 128],
                            ident_f32[:, :])),
                        waits=(wl0 if c4 == 0 else []),
                        inc=("pe" if c4 == 3 else None))
                dve_v = em.emit("vector", (lambda half=half, bank=bank:
                    nc.vector.tensor_scalar(
                        out=qA[:, half * 512:(half + 1) * 512],
                        in0=bank[:, 0:512], scalar1=r_bc[:, 0:1], scalar2=448.0,
                        op0=ALU.mult, op1=ALU.min)),
                    waits=[("pe", pe_v)], drain=True, inc="dve")
                bank_set(bname, "dve", dve_v)

            qwl = []
            if G["xq"][jj % 2] is not None:
                qwl.append(G["xq"][jj % 2])
            xq_ready = em.emit("vector", (lambda jj=jj: nc.vector.tensor_scalar(
                out=xqs[jj % 2], in0=qA, scalar1=-448.0, scalar2=None,
                op0=ALU.max)), waits=qwl, drain=True, inc="dve")
            st.setdefault("xq_ready", {})[jj] = xq_ready

    def emit_tail_ffn(i, jj):
        st = TS_[i]
        base, first, last = tile_base(i)
        xq_ready = st["xq_ready"][jj]
        if True:
            ffn_evac = {}
            for n2 in range(2):
                bank, bname = (P4, "P4") if n2 == 0 else (P5, "P5")
                wl = [("dve", xq_ready)] + bank_wait(bname)
                if first_use["ffn"]:
                    wl.extend(toks_ffn)
                    first_use["ffn"] = False
                for k in range(NKD):
                    em.emit("tensor", (lambda k=k, n2=n2, bank=bank, jj=jj:
                        nc.tensor.matmul(
                            bank[:, 0:512],
                            xqs[jj % 2][:, k * 128:(k + 1) * 128],
                            w_ffn_res[:, k * 1024 + n2 * 512: k * 1024 + n2 * 512 + 512],
                            start=(k == 0), stop=False)),
                        waits=(wl if k == 0 else []))
                pe_v = em.emit("tensor", (lambda n2=n2, bank=bank:
                    nc.tensor.matmul(
                        bank[:, 0:512], ones_row_bf[0:1, :],
                        bp_row_bf[0:1, n2 * 512:(n2 + 1) * 512],
                        start=False, stop=True)),
                    inc="pe")
                ob = outb[jj % 2]
                awl = [("pe", pe_v)]
                if G["outb"][jj % 2] is not None and n2 == 0:
                    awl.append(G["outb"][jj % 2])
                act_v = em.emit("scalar", (lambda n2=n2, bank=bank, ob=ob:
                    nc.scalar.activation(
                        out=ob[:, n2 * 512:(n2 + 1) * 512], in_=bank[:, 0:512],
                        func=AF.Identity, scale=sp_bc[:, 0:1])),
                    waits=awl, drain=True, inc="act")
                bank_set(bname, "act", act_v)
                ffn_evac[n2] = act_v
            G["xq"][jj % 2] = ("pe", em.c("pe"))

            ob = outb[jj % 2]
            r4_done = em.emit("vector", (lambda jj=jj, ob=ob:
                nc.vector.tensor_tensor(
                    ob[:, :], ob[:, :], ln1[:, jj * D:(jj + 1) * D], ALU.add)),
                waits=[("act", max(ffn_evac[0], ffn_evac[1]))],
                drain=True, inc="dve")
            ln_act = emit_ln(lambda ob=ob: ob[:, :], [("dve", r4_done)])
            out_done = em.emit("vector", (lambda ob=ob: nc.vector.tensor_scalar(
                out=ob[:, :], in0=ob[:, :],
                scalar1=stat[:, 14:15], scalar2=stat[:, 6:7],
                op0=ALU.add, op1=ALU.mult)),
                waits=[("act", ln_act)], drain=True, inc="dve")

            def mk_out(base=base, jj=jj, ob=ob):
                dst = bass.AP(y_h.tensor, (base + jj * 128) * D, [[D, 128], [1, D]])
                return nc.gpsimd.dma_start(out=dst, in_=ob[:, :])
            st_v = em.dma("gpsimd", mk_out, stream="st", waits=[("dve", out_done)])
            G["outb"][jj % 2] = st_v
            G["ln1"] = ("dve", out_done)

    import os as _os
    _limit = int(_os.environ.get("MK_LIMIT_TILES", "0")) or (NSEQ * TPS)
    emit_front(0)
    for g0 in range(6):
        issue_gate_pair(0, g0)
    for p in range(NJ):
        emit_gate_part(0, p)
        emit_conv_part(0, p)
    for i in range(_limit):
        emit_mixer(i)
        if i + 1 < _limit:
            emit_front(i + 1)
        emit_ys(i)
        emit_out(i)
        if i + 1 < _limit:
            for g0 in range(6):
                issue_gate_pair(i + 1, g0)
        for jj in range(NJ):
            emit_tail_preA(i, jj)
            if i + 1 < _limit:
                emit_gate_part(i + 1, jj)
            emit_tail_preB(i, jj)
            if jj > 0:
                emit_tail_ffn(i, jj - 1)
            if i + 1 < _limit:
                emit_conv_part(i + 1, jj)
        emit_tail_ffn(i, NJ - 1)

    em.emit("gpsimd", lambda: None, waits=em.stream_total("st"))

    sem_stack = contextlib.ExitStack()
    sems = {name: sem_stack.enter_context(nc.semaphore(name)) for name in em.counts}

    with nc.Block() as block:
        def run_ops(eng_obj, eng_name):
            for fn, waits, inc, inc_n, drain in em.ops[eng_name]:
                if drain:
                    eng_obj.drain()
                for (sname, val) in waits:
                    if val:
                        eng_obj.wait_ge(sems[sname], val)
                inst = fn()
                if inc is not None:
                    if inst is None:
                        inst = eng_obj.drain()
                    inst.then_inc(sems[inc], inc_n)

        @block.sync
        def _(sync):
            run_ops(sync, "sync")

        @block.tensor
        def _(tensor):
            run_ops(tensor, "tensor")

        @block.scalar
        def _(scalar):
            run_ops(scalar, "scalar")

        @block.vector
        def _(vector):
            run_ops(vector, "vector")

        @block.gpsimd
        def _(gpsimd):
            run_ops(gpsimd, "gpsimd")

    es.close()
    sem_stack.close()
    return nc


_NC_CACHE = None


def kernel(**inputs):
    global _NC_CACHE
    import ml_dtypes

    f32 = np.float32
    cc = np.ascontiguousarray
    x = cc(np.asarray(inputs["L0_attn_x"], dtype=f32))
    in_w = np.asarray(inputs["L0_attn_in_w"], dtype=f32)
    in_b = np.asarray(inputs["L0_attn_in_b"], dtype=f32)
    conv_w = np.asarray(inputs["L0_attn_conv_w"], dtype=f32)
    conv_b = np.asarray(inputs["L0_attn_conv_b"], dtype=f32)
    dt_w = np.asarray(inputs["L0_attn_dt_w"], dtype=f32)
    dt_b = np.asarray(inputs["L0_attn_dt_b"], dtype=f32)
    B_w = np.asarray(inputs["L0_attn_B_w"], dtype=f32)
    B_b = np.asarray(inputs["L0_attn_B_b"], dtype=f32)
    C_w = np.asarray(inputs["L0_attn_C_w"], dtype=f32)
    C_b = np.asarray(inputs["L0_attn_C_b"], dtype=f32)
    D_w = np.asarray(inputs["L0_attn_D_w"], dtype=f32)
    D_b = np.asarray(inputs["L0_attn_D_b"], dtype=f32)
    out_w = np.asarray(inputs["L0_attn_out_w"], dtype=f32)
    out_b = np.asarray(inputs["L0_attn_out_b"], dtype=f32)
    ffn_w = np.asarray(inputs["L0_ffn_weight_fp8"], dtype=f32)
    ffn_b = np.asarray(inputs["L0_ffn_bias"], dtype=f32)
    in_scale = np.asarray(inputs["L0_ffn_input_scale"], dtype=f32)
    w_scale = np.asarray(inputs["L0_ffn_weight_scale"], dtype=f32)

    common = {
        "in_wT": cc(in_w.T),
        "dt_wT": cc(dt_w.T),
        "out_wT": cc(out_w.T),
        "ffn_wT": cc(ffn_w.T),
        "bcd_wT": cc(np.concatenate([B_w.T, C_w.T, D_w.T], axis=1)),
        "in_b_pm": cc(in_b.reshape(32, 128).T),
        "dt_b_pm": cc(dt_b.reshape(16, 128).T),
        "conv_b_pm": cc(conv_b.reshape(16, 128).T),
        "conv_w_pm": cc(conv_w[:, 0, :].reshape(16, 128, 3).transpose(1, 0, 2).reshape(128, 48)),
        "bcd_b3": cc(np.stack([B_b, C_b, D_b], axis=1)),
        "out_b_row": cc(out_b[None, :]),
        "ffn_b_row": cc(ffn_b[None, :]),
        "in_scale": cc(in_scale.reshape(1, 1)),
        "w_scale": cc(w_scale.reshape(1, 1)),
        "ident_f32": np.eye(128, dtype=f32),
        "ident_bf": np.eye(128, dtype=ml_dtypes.bfloat16),
        "ones_row_f32": np.ones((1, 128), dtype=f32),
        "ones_row_bf": np.ones((1, 128), dtype=ml_dtypes.bfloat16),
        "ones_col_bf": np.ones((128, 1), dtype=ml_dtypes.bfloat16),
    }
    n_cores = 8
    xr = x.reshape(32, TSEQ, D)
    in_maps = []
    for c in range(n_cores):
        m = dict(common)
        m["x"] = cc(xr[c * NSEQ:(c + 1) * NSEQ].reshape(NTOK, D))
        in_maps.append(m)

    if _NC_CACHE is None:
        _NC_CACHE = build_nc()
    res = run_bass_kernel_spmd(_NC_CACHE, in_maps, core_ids=list(range(n_cores)))
    out = np.concatenate(
        [res.results[c]["y"].reshape(NSEQ, TSEQ, D) for c in range(n_cores)], axis=0)
    return np.ascontiguousarray(out, dtype=np.float32)


# version 20
# speedup vs baseline: 1.0338x; 1.0338x over previous
import contextlib
import numpy as np
import concourse.bass as bass
import concourse.mybir as mybir
from concourse.bass_utils import run_bass_kernel_spmd

F32 = mybir.dt.float32
BF16 = mybir.dt.bfloat16
I32 = mybir.dt.int32
AF = mybir.ActivationFunctionType
ALU = mybir.AluOpType

D = 1024
E = 2048
TSEQ = 2048
NSEQ = 4
TT = 512
TPS = TSEQ // TT
NJ = TT // 128
NKD = D // 128
NKE = E // 128
NTOK = NSEQ * TSEQ

ENGINES = ("sync", "tensor", "scalar", "vector", "gpsimd")


DMA_ROT = 8


class Emitter:
    def __init__(self):
        self.ops = {e: [] for e in ENGINES}
        self.counts = {}
        self.stream_idx = {}

    def sem(self, name):
        self.counts.setdefault(name, 0)

    def emit(self, eng, fn, waits=(), inc=None, inc_n=1, drain=False):
        val = None
        if inc is not None:
            self.counts[inc] = self.counts.get(inc, 0) + inc_n
            val = self.counts[inc]
        self.ops[eng].append((fn, tuple(waits), inc, inc_n, drain))
        return val

    def dma(self, eng, fn, stream, waits=()):
        idx = self.stream_idx.get(stream, 0)
        self.stream_idx[stream] = idx + 1
        sem = f"{stream}{idx % DMA_ROT}"
        self.sem(sem)
        w = list(waits)
        prev = 16 * (idx // DMA_ROT)
        if prev > 0:
            w.append((sem, prev))
        self.counts[sem] = self.counts.get(sem, 0) + 16
        self.ops[eng].append((fn, tuple(w), sem, 16, False))
        return (sem, 16 * (idx // DMA_ROT + 1))

    def stream_total(self, stream):
        n = self.stream_idx.get(stream, 0)
        toks = []
        for r in range(DMA_ROT):
            cnt = (n - r + DMA_ROT - 1) // DMA_ROT
            if cnt > 0:
                toks.append((f"{stream}{r}", 16 * cnt))
        return toks

    def c(self, name):
        return self.counts.get(name, 0)


def build_nc():
    nc = bass.Bass("TRN2", target_bir_lowering=False)

    def din(name, shape, dt=F32):
        return nc.dram_tensor(name, shape, dt, kind="ExternalInput").ap()

    x_h = din("x", [NTOK, D])
    w_in_h = din("in_wT", [D, 2 * E])
    w_dt_h = din("dt_wT", [E, E])
    w_out_h = din("out_wT", [E, D])
    w_ffn_h = din("ffn_wT", [D, D])
    w_bcd_h = din("bcd_wT", [E, 48])
    in_b_pm_h = din("in_b_pm", [128, 32])
    dt_b_pm_h = din("dt_b_pm", [128, 16])
    conv_b_pm_h = din("conv_b_pm", [128, 16])
    conv_w_pm_h = din("conv_w_pm", [128, 48])
    bcd_b3_h = din("bcd_b3", [16, 3])
    out_b_row_h = din("out_b_row", [1, D])
    ffn_b_row_h = din("ffn_b_row", [1, D])
    in_scale_h = din("in_scale", [1, 1])
    w_scale_h = din("w_scale", [1, 1])
    ident_f32_h = din("ident_f32", [128, 128])
    ident_bf_h = din("ident_bf", [128, 128], BF16)
    ones_row_f32_h = din("ones_row_f32", [1, 128])
    ones_row_bf_h = din("ones_row_bf", [1, 128], BF16)
    ones_col_bf_h = din("ones_col_bf", [128, 1], BF16)

    y_h = nc.dram_tensor("y", [NTOK, D], F32, kind="ExternalOutput").ap()

    sc_in = nc.dram_tensor("sc_in", [32 * 128 * 1024], BF16, kind="Internal").ap()
    sc_dt = nc.dram_tensor("sc_dt", [16 * 128 * 2048], BF16, kind="Internal").ap()
    sc_out = nc.dram_tensor("sc_out", [2 * 128 * 8192], BF16, kind="Internal").ap()

    em = Emitter()
    for s in ("pe", "act", "dve", "gp"):
        em.sem(s)

    es = contextlib.ExitStack()

    def sb(name, shape, dt=F32):
        return es.enter_context(nc.sbuf_tensor(name, shape, dt))

    def psumt(name, shape, dt=F32):
        return es.enter_context(nc.psum_tensor(name, shape, dt))

    xA = sb("xA", [128, 2 * D], BF16)
    xres = sb("xres", [128, 4 * D])
    x_fm = sb("x_fm", [128, NKD * (TT + 1)], BF16)
    xm = sb("xm", [128, NKE * (TT + 2)], BF16)
    xc = sb("xc", [128, NKE * TT], BF16)
    g_sb = sb("g_sb", [128, NKE * TT], BF16)
    cbuf = [[sb(f"cb{s}{t}", [128, TT], BF16) for t in range(2)] for s in range(2)]
    cvt = sb("cvt", [128, TT])
    sp_bf = [sb("sp0", [128, TT], BF16), sb("sp1", [128, TT], BF16)]
    w_in_sl = [sb(f"win{t}", [128, 2048], BF16) for t in range(4)]
    w_dt_sl = [sb(f"wdt{t}", [128, 2048], BF16) for t in range(6)]
    w_out_sl = [sb(f"wout{t}", [128, 4096], BF16) for t in range(2)]
    w_ffn_res = sb("wffn", [128, 8 * 1024], mybir.dt.float8e4)
    w_bcd_res = sb("wbcd", [128, 16 * 48], BF16)
    decay16 = sb("decay16", [16, TT])
    u16 = sb("u16", [16, TT])
    c16 = sb("c16", [16, TT])
    cs16 = sb("cs16", [16, TT])
    du16 = cs16
    st16s = sb("st16a", [16, TT])
    state_col = sb("state_col", [16, 1])
    ys_tm = sb("ys_tm", [128, NJ])
    r2f = sb("r2f", [128, NJ * D])
    ln1 = r2f
    scrF = sb("scrF", [128, 1024])
    qA = scrF[:, 0:1024]
    outb = [sb("outb0", [128, D]), sb("outb1", [128, D])]
    stat = sb("stat", [128, 16])
    xq8 = sb("xq8", [128, 2048], mybir.dt.float8e4)
    xqs = [xq8[:, 0:1024], xq8[:, 1024:2048]]
    in_b_pm = sb("c_in_b_pm", [128, 32])
    dt_b_pm = sb("c_dt_b_pm", [128, 16])
    conv_b_pm = sb("c_conv_b_pm", [128, 16])
    conv_w_pm = sb("c_conv_w_pm", [128, 48])
    bcd_b3 = sb("c_bcd_b3", [16, 3])
    out_b_row = sb("c_out_b_row", [1, D])
    ffn_b_row = sb("c_ffn_b_row", [1, D])
    outb_bc = sb("outb_bc", [128, D])
    bp_row_bf = sb("bp_row_bf", [1, D], BF16)
    sc_sc = sb("sc_sc", [1, 8])
    r_bc = sb("r_bc", [128, 1])
    sp_bc = sb("sp_bc", [128, 1])
    ident_f32 = sb("c_ident_f32", [128, 128])
    ident_bf = sb("c_ident_bf", [128, 128], BF16)
    ones_row_f32 = sb("c_ones_row_f32", [1, 128])
    ones_row_bf = sb("c_ones_row_bf", [1, 128], BF16)
    ones_col_bf = sb("c_ones_col_bf", [128, 1], BF16)

    xh_sb = out_b_row
    decay_t = ffn_b_row[0:1, 0:512]
    decay_row = ffn_b_row[0:1, 512:1024]

    P0 = psumt("P0", [128, 512])
    P1 = psumt("P1", [128, 512])
    P2 = psumt("P2", [128, 512])
    P3 = psumt("P3", [128, 512])
    P4 = psumt("P4", [128, 512])
    P5 = psumt("P5", [128, 512])
    PT = psumt("PT", [128, 512])
    PS1 = psumt("PS1", [128, 512])
    PMAIN = [P0, P1]
    PHALO = [P2, P3]

    bank_free = {}

    def bank_wait(bname):
        w = bank_free.pop(bname, None)
        return [w] if w is not None else []

    def bank_set(bname, sem, cnt):
        bank_free[bname] = (sem, cnt)

    const_loads = [
        (in_b_pm, in_b_pm_h), (dt_b_pm, dt_b_pm_h), (conv_b_pm, conv_b_pm_h),
        (conv_w_pm, conv_w_pm_h), (bcd_b3, bcd_b3_h), (out_b_row, out_b_row_h),
        (ffn_b_row, ffn_b_row_h), (ident_f32, ident_f32_h),
        (ident_bf, ident_bf_h),
        (ones_row_f32, ones_row_f32_h), (ones_row_bf, ones_row_bf_h),
        (ones_col_bf, ones_col_bf_h),
    ]
    cl_tok = None
    for dst, srct in const_loads:
        cl_tok = em.dma("sync", (lambda d=dst, s=srct: nc.sync.dma_start(out=d[:, :], in_=s[:, :])),
                        stream="cl")
    cl_tok = em.dma("sync", lambda: nc.sync.dma_start(out=sc_sc[0:1, 0:1], in_=in_scale_h[:, :]), stream="cl")
    cl_tok = em.dma("sync", lambda: nc.sync.dma_start(out=sc_sc[0:1, 1:2], in_=w_scale_h[:, :]), stream="cl")
    cl_all = em.stream_total("cl")

    em.emit("vector", lambda: nc.vector.tensor_tensor(
        sc_sc[0:1, 2:3], sc_sc[0:1, 0:1], sc_sc[0:1, 1:2], ALU.mult),
        waits=cl_all, drain=True)
    em.emit("vector", lambda: nc.vector.reciprocal(sc_sc[0:1, 3:4], sc_sc[0:1, 2:3]), drain=True)
    em.emit("vector", lambda: nc.vector.reciprocal(sc_sc[0:1, 4:5], sc_sc[0:1, 0:1]), drain=True)
    em.emit("vector", lambda: nc.vector.tensor_scalar(
        out=bp_row_bf[:, :], in0=ffn_b_row[:, :], scalar1=sc_sc[0:1, 3:4],
        scalar2=None, op0=ALU.mult), drain=True)
    dve_prep = em.emit("vector", lambda: nc.vector.tensor_copy(
        sc_sc[0:1, 5:6], sc_sc[0:1, 0:1]), drain=True, inc="dve")
    em.emit("tensor", lambda: nc.tensor.matmul(
        PT[:, 0:1], ones_row_f32[0:1, :], sc_sc[0:1, 4:5], start=True, stop=True),
        waits=[("dve", dve_prep)])
    pe_bc = em.emit("tensor", lambda: nc.tensor.matmul(
        PT[:, 1:2], ones_row_f32[0:1, :], sc_sc[0:1, 2:3], start=True, stop=True),
        inc="pe")
    em.emit("vector", lambda: nc.vector.tensor_copy(r_bc[:, :], PT[:, 0:1]),
            waits=[("pe", pe_bc)], drain=True)
    dve_bc = em.emit("vector", lambda: nc.vector.tensor_copy(sp_bc[:, :], PT[:, 1:2]),
                     drain=True, inc="dve")
    pe_ob = em.emit("tensor", lambda: nc.tensor.matmul(
        PS1[:, 0:512], ones_row_f32[0:1, :], out_b_row[0:1, 0:512],
        start=True, stop=True), waits=[("dve", dve_prep)], inc="pe")
    em.emit("vector", lambda: nc.vector.tensor_copy(outb_bc[:, 0:512], PS1[:, 0:512]),
            waits=[("pe", pe_ob)], drain=True, inc="dve")
    dve_ob1 = em.c("dve")
    pe_ob2 = em.emit("tensor", lambda: nc.tensor.matmul(
        PS1[:, 0:512], ones_row_f32[0:1, :], out_b_row[0:1, 512:1024],
        start=True, stop=True), waits=[("dve", dve_ob1)], inc="pe")
    dve_ob2 = em.emit("vector", lambda: nc.vector.tensor_copy(
        outb_bc[:, 512:1024], PS1[:, 0:512]),
        waits=[("pe", pe_ob2)], drain=True, inc="dve")
    bank_set("PT", "dve", dve_bc)
    bank_set("PS1", "dve", dve_ob2)
    gate_outb_row = ("pe", pe_ob2)
    prep_outb_done = ("dve", dve_ob2)

    X0_TOKS = {}
    for jx in range(2):
        def mk_x0(jx=jx):
            srcp = bass.AP(x_h.tensor, (jx * 128) * D, [[D, 128], [1, D]])
            dst = bass.AP(xA, (jx % 2) * D, [[2 * D, 128], [1, D]])
            return nc.gpsimd.dma_start(out=dst, in_=srcp)
        X0_TOKS[jx] = em.dma("gpsimd", mk_x0, stream="x")

    toks_in = []
    for k in range(NKD):
        for ch in range(2):
            t = em.dma("gpsimd", (lambda k=k, ch=ch: nc.gpsimd.dma_start(
                out=bass.AP(sc_in.tensor, k * 128 + ch * 1024,
                            [[2048, 128], [128 * 2048, 16], [1, 128]]),
                in_=bass.AP(w_in_h.tensor, k * 128 * 4096 + ch * 128,
                            [[4096, 128], [256, 16], [1, 128]]))),
                stream="pr")
            toks_in.append(t)
    toks_dt = []
    for k in range(NKE):
        t = em.dma("gpsimd", (lambda k=k: nc.gpsimd.dma_start(
            out=bass.AP(sc_dt.tensor, k * 128, [[2048, 128], [128 * 2048, 16], [1, 128]]),
            in_=bass.AP(w_dt_h.tensor, k * 128 * 2048, [[2048, 128], [128, 16], [1, 128]]))),
            stream="pr")
        toks_dt.append(t)
    toks_out = []
    for k in range(NKE):
        t = em.dma("gpsimd", (lambda k=k: nc.gpsimd.dma_start(
            out=bass.AP(sc_out.tensor, k * 512, [[8192, 128], [128 * 8192, 2], [1, 512]]),
            in_=bass.AP(w_out_h.tensor, k * 128 * 1024, [[1024, 128], [512, 2], [1, 512]]))),
            stream="pr")
        toks_out.append(t)
    toks_ffn = []
    ffn_stage_done = [None, None]
    for k in range(NKD):
        stg = w_out_sl[k % 2]
        wl = []
        if ffn_stage_done[k % 2] is not None:
            wl.append(("dve", ffn_stage_done[k % 2]))
        t = em.dma("gpsimd", (lambda k=k, stg=stg: nc.gpsimd.dma_start(
            out=stg[:, 0:1024],
            in_=w_ffn_h[k * 128:(k + 1) * 128, :])),
            stream="pr", waits=wl)
        dve_c = em.emit("vector", (lambda k=k, stg=stg: nc.vector.tensor_copy(
            w_ffn_res[:, k * 1024:(k + 1) * 1024], stg[:, 0:1024])),
            waits=[t], drain=True, inc="dve")
        ffn_stage_done[k % 2] = dve_c
        toks_ffn.append(("dve", dve_c))
    ffn_stage_toks = [("dve", ffn_stage_done[0]), ("dve", ffn_stage_done[1])]
    toks_bcd = []
    for k in range(NKE):
        t = em.dma("gpsimd", (lambda k=k: nc.gpsimd.dma_start(
            out=w_bcd_res[:, k * 48:(k + 1) * 48],
            in_=w_bcd_h[k * 128:(k + 1) * 128, :])),
            stream="pr")
        toks_bcd.append(t)
    toks_in = toks_in[-DMA_ROT:]
    toks_dt = toks_dt[-DMA_ROT:]
    toks_out = toks_out[-DMA_ROT:]
    toks_ffn = toks_ffn[-DMA_ROT:]
    toks_bcd = toks_bcd[-DMA_ROT:]

    G = {
        "x_slot": [None, None],
        "xres": [None] * 4,
        "xfm": None,
        "inslab": [None] * 4,
        "dtslab": [None] * 6,
        "slab_ring": [0],
        "outslab": ffn_stage_toks[:],
        "sp": [None, None],
        "cbuf": [None, None],
        "outb": [None, None],
        "xq": [None, None],
        "r2f": None,
        "ln1": None,
        "g": None,
        "xc": None,
        "xm_taps": None,
        "cs": None,
        "scan": None,
    }
    first_use = {"in": True, "dt": True, "out": True, "ffn": True, "bcd": True}
    TS_ = {}

    def tile_base(i):
        s, ti = divmod(i, TPS)
        return s * TSEQ + ti * TT, (ti == 0), (ti == TPS - 1)

    def emit_front(i):
        base, first, last = tile_base(i)
        st = TS_.setdefault(i, {})
        x_toks = {}

        def load_x(jx):
            if i == 0 and jx < 2:
                x_toks[jx] = X0_TOKS[jx]
                return
            slot = (i * NJ + jx) % 2
            wl = [G["x_slot"][slot]] if G["x_slot"][slot] is not None else []
            def mk_x(base=base, jx=jx, slot=slot):
                srcp = bass.AP(x_h.tensor, (base + jx * 128) * D, [[D, 128], [1, D]])
                dst = bass.AP(xA, slot * D, [[2 * D, 128], [1, D]])
                return nc.gpsimd.dma_start(out=dst, in_=srcp)
            x_toks[jx] = em.dma("gpsimd", mk_x, stream="x", waits=wl)
        load_x(0)
        load_x(1)
        if not last:
            def mk_xh(base=base):
                return nc.sync.dma_start(
                    out=xh_sb[0:1, :], in_=x_h[base + TT:base + TT + 1, :])
            st["xh_tok"] = em.dma("sync", mk_xh, stream="xh",
                                  waits=([gate_outb_row] if i == 0 else []))
        BANKS4 = [P0, P1, P2, P3]
        for jj in range(NJ):
            slot = (i * NJ + jj) % 2
            last_pe = None
            for c4 in range(2):
                bidx = (jj * 2 + c4) % 4
                bank, bname = BANKS4[bidx], f"P{bidx}"
                wl = [x_toks[jj]] + bank_wait(bname)
                pe_v = None
                for q in range(4):
                    c = c4 * 4 + q
                    pe_v = em.emit("tensor", (lambda c=c, q=q, bank=bank, slot=slot:
                        nc.tensor.transpose(
                            bank[:, q * 64:(q + 1) * 64].bitcast(BF16),
                            bass.AP(xA, slot * D + c * 128, [[2 * D, 128], [1, 128]]),
                            ident_bf[:, :])),
                        waits=(wl if q == 0 else []), inc=("pe" if q == 3 else None))
                awl = [("pe", pe_v)]
                if jj == 0 and c4 == 0 and G["xfm"] is not None:
                    awl.append(G["xfm"])
                act_v = em.emit("scalar", (lambda jj=jj, c4=c4, bank=bank:
                    nc.scalar.copy(
                        bass.AP(x_fm, c4 * 4 * (TT + 1) + jj * 128,
                                [[NKD * (TT + 1), 128], [TT + 1, 4], [1, 128]]),
                        bank[:, 0:256].bitcast(BF16))),
                    waits=awl, drain=True, inc="act")
                bank_set(bname, "act", act_v)
                last_pe = pe_v
            G["x_slot"][slot] = ("pe", last_pe)
            if jj + 2 < NJ:
                load_x(jj + 2)
        if not last:
            for c4 in range(2):
                bank, bname = BANKS4[c4], f"P{c4}"
                wl = [st["xh_tok"]] + bank_wait(bname)
                pe_v = None
                for q in range(4):
                    c = c4 * 4 + q
                    pe_v = em.emit("tensor", (lambda c=c, q=q, bank=bank:
                        nc.tensor.transpose(
                            bank[:, q:q + 1],
                            xh_sb[0:1, c * 128:(c + 1) * 128],
                            ident_f32[0:1, 0:1])),
                        waits=(wl if q == 0 else []), inc=("pe" if q == 3 else None))
                act_v = em.emit("scalar", (lambda c4=c4, bank=bank:
                    nc.scalar.copy(
                        bass.AP(x_fm, c4 * 4 * (TT + 1) + TT,
                                [[NKD * (TT + 1), 128], [TT + 1, 4], [1, 1]]),
                        bank[:, 0:4])),
                    waits=[("pe", pe_v)], drain=True, inc="act")
                bank_set(bname, "act", act_v)
        st["xfm_ready"] = em.c("act")

        wl = [G["xm_taps"]] if G["xm_taps"] is not None else []
        if first:
            def mk_hl():
                return nc.vector.memset(
                    bass.AP(xm, 0, [[NKE * (TT + 2), 128], [TT + 2, NKE], [1, 1]]), 0.0)
        else:
            def mk_hl():
                srcp = bass.AP(xm, TT, [[NKE * (TT + 2), 128], [TT + 2, NKE], [1, 1]])
                dst = bass.AP(xm, 0, [[NKE * (TT + 2), 128], [TT + 2, NKE], [1, 1]])
                return nc.vector.tensor_copy(dst, srcp)
        dve_hl = em.emit("vector", mk_hl, waits=wl, drain=True, inc="dve")
        if last:
            dve_hl = em.emit("vector", lambda: nc.vector.memset(
                bass.AP(xm, TT + 1, [[NKE * (TT + 2), 128], [TT + 2, NKE], [1, 1]]), 0.0),
                drain=True, inc="dve")
        st["dve_hl"] = dve_hl

        st["inproj_act"] = {}
        for mp in range(8):
            b = mp % 4
            wl = [G["inslab"][b]] if G["inslab"][b] is not None else []
            if first_use["in"]:
                wl.extend(toks_in)
                first_use["in"] = False
            def mk_sl(mp=mp, b=b):
                srcp = bass.AP(sc_in.tensor, mp * 128 * 2048, [[2048, 128], [1, 2048]])
                dstp = bass.AP(w_in_sl[b], 0, [[2048, 128], [1, 2048]])
                return nc.sync.dma_start(out=dstp, in_=srcp)
            sl_v = em.dma("sync", mk_sl, stream="wi", waits=wl)
            pe_v = None
            for half in range(2):
                m = 2 * mp + half
                bname = "P0" if m % 2 == 0 else "P1"
                wl = [sl_v, ("act", st["xfm_ready"])] + bank_wait(bname)
                for k in range(NKD):
                    pe_v = em.emit("tensor", (lambda k=k, b=b, half=half, m=m:
                        nc.tensor.matmul(
                            PMAIN[m % 2][:, 0:512],
                            w_in_sl[b][:, half * 1024 + k * 128: half * 1024 + (k + 1) * 128],
                            x_fm[:, k * (TT + 1): k * (TT + 1) + TT],
                            start=(k == 0), stop=(k == NKD - 1))),
                        waits=(wl if k == 0 else []), inc=("pe" if k == NKD - 1 else None))
                act_v = em.emit("scalar", (lambda m=m:
                    nc.scalar.activation(
                        out=xm[:, m * (TT + 2) + 1: m * (TT + 2) + 1 + TT],
                        in_=PMAIN[m % 2][:, 0:512], func=AF.Identity,
                        bias=in_b_pm[:, m:m + 1], scale=1.0)),
                    waits=[("pe", pe_v), ("dve", dve_hl)], drain=True, inc="act")
                bank_set(bname, "act", act_v)
                st["inproj_act"][m] = act_v
                if not last:
                    hname = "P2" if m % 2 == 0 else "P3"
                    hbank = PHALO[m % 2]
                    hwl = bank_wait(hname)
                    for k in range(NKD):
                        pe_v = em.emit("tensor", (lambda k=k, b=b, half=half, hbank=hbank:
                            nc.tensor.matmul(
                                hbank[:, 0:1],
                                w_in_sl[b][:, half * 1024 + k * 128: half * 1024 + (k + 1) * 128],
                                x_fm[:, k * (TT + 1) + TT: k * (TT + 1) + TT + 1],
                                start=(k == 0), stop=(k == NKD - 1))),
                            waits=(hwl if k == 0 else []),
                            inc=("pe" if k == NKD - 1 else None))
                    act_v2 = em.emit("scalar", (lambda m=m, hbank=hbank:
                        nc.scalar.activation(
                            out=xm[:, m * (TT + 2) + TT + 1: m * (TT + 2) + TT + 2],
                            in_=hbank[:, 0:1], func=AF.Identity,
                            bias=in_b_pm[:, m:m + 1], scale=1.0)),
                        waits=[("pe", pe_v)], drain=True, inc="act")
                    bank_set(hname, "act", act_v2)
                    st["inproj_act"][m] = act_v2
            G["inslab"][b] = ("pe", pe_v)

    def ring_next():
        r = G["slab_ring"][0]
        G["slab_ring"][0] = (r + 1) % 6
        return r

    def issue_gate_pair(i, g):
        st = TS_[i]
        m0 = 16 + 2 * g
        b = ring_next()
        wl = [G["dtslab"][b]] if G["dtslab"][b] is not None else []
        def mk_sl(g=g, b=b):
            srcp = bass.AP(sc_in.tensor, (8 + g) * 128 * 2048,
                           [[2048, 128], [1, 2048]])
            dstp = bass.AP(w_dt_sl[b], 0, [[2048, 128], [1, 2048]])
            return nc.sync.dma_start(out=dstp, in_=srcp)
        st.setdefault("gate_sl", {})[g] = (
            em.dma("sync", mk_sl, stream="wg", waits=wl), b)
        G["dtslab"][b] = None

    def emit_gate_part(i, part):
        base, first, last = tile_base(i)
        st = TS_[i]
        for pair in range(2):
            g = 2 * part + pair
            sl_v, b = st["gate_sl"][g]
            pe_v = None
            for half in range(2):
                mg = 4 * part + 2 * pair + half
                gbank = [P0, P1, P2, P3][mg % 4]
                bname = f"P{mg % 4}"
                wl = [sl_v, ("act", st["xfm_ready"])] + bank_wait(bname)
                for k in range(NKD):
                    pe_v = em.emit("tensor", (lambda k=k, b=b, half=half, gbank=gbank:
                        nc.tensor.matmul(
                            gbank[:, 0:512],
                            w_dt_sl[b][:, half * 1024 + k * 128: half * 1024 + (k + 1) * 128],
                            x_fm[:, k * (TT + 1): k * (TT + 1) + TT],
                            start=(k == 0), stop=(k == NKD - 1))),
                        waits=(wl if k == 0 else []), inc=("pe" if k == NKD - 1 else None))
                gwl = [("pe", pe_v)]
                if mg == 0 and G["g"] is not None:
                    gwl.append(G["g"])
                act_v = em.emit("scalar", (lambda mg=mg, gbank=gbank:
                    nc.scalar.activation(
                        out=g_sb[:, mg * TT:(mg + 1) * TT],
                        in_=gbank[:, 0:512], func=AF.Silu,
                        bias=in_b_pm[:, 16 + mg:16 + mg + 1], scale=1.0)),
                    waits=gwl, drain=True, inc="act")
                bank_set(bname, "act", act_v)
            G["dtslab"][b] = ("pe", pe_v)
            if g + 6 < 8:
                issue_gate_pair(i, g + 6)
        if part == 3:
            G["xfm"] = ("pe", em.c("pe"))

    def emit_conv_part(i, part):
        st = TS_[i]
        if part == 0:
            st.setdefault("conv_act", {})
        for m in range(4 * part, 4 * part + 4):
            cb = cbuf[m % 2]
            base_c = m * (TT + 2)
            wl0 = [("act", st["inproj_act"][m])]
            if G["cbuf"][m % 2] is not None:
                wl0.append(G["cbuf"][m % 2])
            em.emit("vector", (lambda m=m, cb=cb, base_c=base_c:
                nc.vector.tensor_scalar(
                    out=cb[0][:, :], in0=xm[:, base_c: base_c + TT],
                    scalar1=conv_w_pm[:, 3 * m: 3 * m + 1], scalar2=None,
                    op0=ALU.mult)),
                waits=wl0, drain=True)
            em.emit("vector", (lambda m=m, cb=cb, base_c=base_c:
                nc.vector.scalar_tensor_tensor(
                    out=cb[0][:, :], in0=xm[:, base_c + 1: base_c + 1 + TT],
                    scalar=conv_w_pm[:, 3 * m + 1: 3 * m + 2], in1=cb[0][:, :],
                    op0=ALU.mult, op1=ALU.add)),
                drain=True)
            dve_v = em.emit("vector", (lambda m=m, cb=cb, base_c=base_c:
                nc.vector.scalar_tensor_tensor(
                    out=cb[0][:, :], in0=xm[:, base_c + 2: base_c + 2 + TT],
                    scalar=conv_w_pm[:, 3 * m + 2: 3 * m + 3], in1=cb[0][:, :],
                    op0=ALU.mult, op1=ALU.add)),
                drain=True, inc="dve")
            xwl = [("dve", dve_v)]
            if m == 0 and G["xc"] is not None:
                xwl.append(G["xc"])
            act_v = em.emit("scalar", (lambda m=m, cb=cb:
                nc.scalar.activation(
                    out=xc[:, m * TT:(m + 1) * TT], in_=cb[0][:, :], func=AF.Silu,
                    bias=conv_b_pm[:, m:m + 1], scale=1.0)),
                waits=xwl, drain=True, inc="act")
            st["conv_act"][m] = act_v
            G["cbuf"][m % 2] = ("act", act_v)
        if part == 3:
            st["xc_ready"] = em.c("act")
            G["xm_taps"] = ("dve", em.c("dve"))

    def emit_mixer(i):
        st = TS_[i]
        base, first, last = tile_base(i)
        xc_ready = st["xc_ready"]
        for jjx in range(NJ):
            prep_xres(i, jjx)
        pend_ones = None
        pe_v2 = None
        for m in range(NKE):
            b = m % 2
            sbi = ring_next()
            wl = [G["dtslab"][sbi]] if G["dtslab"][sbi] is not None else []
            if first_use["dt"]:
                wl.extend(toks_dt)
                first_use["dt"] = False
            def mk_sl(m=m, sbi=sbi):
                srcp = bass.AP(sc_dt.tensor, m * 128 * 2048, [[2048, 128], [1, 2048]])
                return nc.sync.dma_start(out=w_dt_sl[sbi][:, :], in_=srcp)
            sl_v = em.dma("sync", mk_sl, stream="wd", waits=wl)
            bname = "P0" if b == 0 else "P1"
            wl = [sl_v, ("act", xc_ready)] + bank_wait(bname)
            pe_v = None
            for k in range(NKE):
                pe_v = em.emit("tensor", (lambda k=k, b=b, sbi=sbi:
                    nc.tensor.matmul(
                        PMAIN[b][:, 0:512],
                        w_dt_sl[sbi][:, k * 128:(k + 1) * 128],
                        xc[:, k * TT:(k + 1) * TT],
                        start=(k == 0), stop=(k == NKE - 1))),
                    waits=(wl if k == 0 else []), inc=("pe" if k == NKE - 1 else None))
            G["dtslab"][sbi] = ("pe", pe_v)
            if pend_ones is not None:
                pe_v2 = em.emit("tensor", pend_ones[0], waits=pend_ones[1], inc="pe")
                G["sp"][pend_ones[2]] = ("pe", pe_v2)
            swl = [("pe", pe_v)]
            if G["sp"][b] is not None:
                swl.append(G["sp"][b])
            em.emit("scalar", (lambda b=b, m=m:
                nc.scalar.activation(
                    out=cvt[:, :], in_=PMAIN[b][:, 0:512], func=AF.Exp,
                    bias=dt_b_pm[:, m:m + 1], scale=1.0)),
                waits=swl, drain=True)
            act_v = em.emit("scalar", (lambda b=b:
                nc.scalar.activation(
                    out=sp_bf[b][:, :], in_=cvt[:, :], func=AF.Ln,
                    bias=1.0, scale=1.0)),
                drain=True, inc="act")
            bank_set(bname, "act", act_v)
            owl = [("act", act_v)]
            if m == 0:
                owl += bank_wait("PS1")
            pend_ones = ((lambda b=b, m=m: nc.tensor.matmul(
                PS1[0:1, 0:512], ones_col_bf[:, 0:1], sp_bf[b][:, :],
                start=(m == 0), stop=(m == NKE - 1))), owl, b)
        pe_v2 = em.emit("tensor", pend_ones[0], waits=pend_ones[1], inc="pe")
        G["sp"][pend_ones[2]] = ("pe", pe_v2)
        dtsum_done = pe_v2

        dve_v = em.emit("vector", lambda: nc.vector.tensor_scalar(
            out=decay_t, in0=PS1[0:1, 0:512],
            scalar1=float(-1.0 / 2048.0), scalar2=-10.0,
            op0=ALU.mult, op1=ALU.max),
            waits=[("pe", dtsum_done)], drain=True, inc="dve")
        bank_set("PS1", "dve", dve_v)
        act_v = em.emit("scalar", lambda: nc.scalar.activation(
            out=decay_row, in_=decay_t, func=AF.Exp),
            waits=[("dve", dve_v)], drain=True, inc="act")
        wl = [("act", act_v)] + bank_wait("PT")
        pe_v = em.emit("tensor", lambda: nc.tensor.matmul(
            PT[:, 0:512], ones_row_f32[0:1, :], decay_row,
            start=True, stop=True),
            waits=wl, inc="pe")
        dve_v = em.emit("vector", lambda: nc.vector.tensor_copy(
            decay16[:, :], PT[0:16, 0:512]),
            waits=[("pe", pe_v)], drain=True, inc="dve")
        bank_set("PT", "dve", dve_v)

        bcd_banks = [("P2", P2, 0), ("P3", P3, 16), ("P4", P4, 32)]
        bcd_done = {}
        for bn, bank, off in bcd_banks:
            wl = [("act", xc_ready)] + bank_wait(bn)
            if first_use["bcd"]:
                wl.extend(toks_bcd)
                first_use["bcd"] = False
            pe_v = None
            for k in range(NKE):
                pe_v = em.emit("tensor", (lambda k=k, bank=bank, off=off:
                    nc.tensor.matmul(
                        bank[0:16, 0:512],
                        w_bcd_res[:, k * 48 + off: k * 48 + off + 16],
                        xc[:, k * TT:(k + 1) * TT],
                        start=(k == 0), stop=(k == NKE - 1))),
                    waits=(wl if k == 0 else []), inc=("pe" if k == NKE - 1 else None))
            bcd_done[bn] = pe_v

        em.emit("vector", lambda: nc.vector.tensor_scalar(
            out=u16[:, :], in0=P2[0:16, 0:512], scalar1=bcd_b3[:, 0:1],
            scalar2=None, op0=ALU.add),
            waits=[("pe", bcd_done["P2"])], drain=True)
        em.emit("vector", lambda: nc.vector.tensor_scalar(
            out=du16[:, :], in0=P4[0:16, 0:512], scalar1=bcd_b3[:, 2:3],
            scalar2=None, op0=ALU.add),
            waits=[("pe", bcd_done["P4"])] + ([G["cs"]] if G["cs"] is not None else []),
            drain=True)
        em.emit("vector", lambda: nc.vector.tensor_tensor(
            u16[:, :], u16[:, :], du16[:, :], ALU.mult), drain=True)
        dve_c16 = em.emit("vector", lambda: nc.vector.tensor_scalar(
            out=c16[:, :], in0=P3[0:16, 0:512], scalar1=bcd_b3[:, 1:2],
            scalar2=None, op0=ALU.add),
            waits=[("pe", bcd_done["P3"])], drain=True, inc="dve")
        bank_set("P2", "dve", dve_c16)
        bank_set("P3", "dve", dve_c16)
        bank_set("P4", "dve", dve_c16)

        if first:
            em.emit("vector", lambda: nc.vector.tensor_tensor_scan(
                st16s[:, :], decay16[:, :], u16[:, :], 0.0,
                ALU.mult, ALU.add), drain=True)
        else:
            em.emit("vector", lambda: nc.vector.tensor_tensor_scan(
                st16s[:, :], decay16[:, :], u16[:, :],
                state_col[:, 0:1], ALU.mult, ALU.add), drain=True)
        em.emit("vector", lambda: nc.vector.tensor_copy(
            state_col[:, 0:1], st16s[:, TT - 1:TT]), drain=True)
        cs_ready = em.emit("vector", lambda: nc.vector.tensor_tensor(
            cs16[:, :], c16[:, :], st16s[:, :], ALU.mult),
            drain=True, inc="dve")
        st["cs_ready"] = cs_ready

    def emit_ys(i):
        st = TS_[i]
        cs_ready = st["cs_ready"]
        ys_ready = {}
        last_cs_pe = None
        for jj in range(NJ):
            b = jj % 2
            bank, bname = PMAIN[b], ("P0" if b == 0 else "P1")
            wl = [("dve", cs_ready)] + bank_wait(bname)
            pe_v = em.emit("tensor", (lambda jj=jj, bank=bank:
                nc.tensor.transpose(
                    bank[:, 0:16], cs16[0:16, jj * 128:(jj + 1) * 128],
                    ident_f32[0:16, 0:16])),
                waits=wl, inc="pe")
            dve_v = em.emit("vector", (lambda jj=jj, bank=bank:
                nc.vector.tensor_reduce(
                    out=ys_tm[:, jj:jj + 1], in_=bank[:, 0:16],
                    axis=mybir.AxisListType.X, op=ALU.add)),
                waits=[("pe", pe_v)], drain=True, inc="dve")
            ys_ready[jj] = dve_v
            bank_set(bname, "dve", dve_v)
            last_cs_pe = pe_v
        G["cs"] = ("pe", last_cs_pe)
        st["ys_ready"] = ys_ready

    def emit_out(i):
        st = TS_[i]
        base, first, last = tile_base(i)
        ys_ready = st["ys_ready"]
        evac_out = {}
        pe_last = {}
        PBANKS4 = [("P0", P0), ("P1", P1), ("P2", P2), ("P3", P3)]
        for n2 in range(2):
            for h in range(2):
                ob_i = (n2 * 2 + h) % 2
                wl = [G["outslab"][ob_i]] if G["outslab"][ob_i] is not None else []
                if first_use["out"]:
                    wl.extend(toks_out)
                    first_use["out"] = False
                def mk_sl(n2=n2, h=h, ob_i=ob_i):
                    srcp = bass.AP(sc_out.tensor, n2 * 128 * 8192 + h * 4096,
                                   [[8192, 128], [1, 4096]])
                    return nc.scalar.dma_start(out=w_out_sl[ob_i][:, :], in_=srcp)
                sl_v = em.dma("scalar", mk_sl, stream="wo", waits=wl)
                for jj in range(NJ):
                    bname, bank = PBANKS4[jj]
                    wl = [sl_v]
                    if h == 0:
                        wl += bank_wait(bname)
                    for k8 in range(8):
                        k = h * 8 + k8
                        v = em.emit("tensor", (lambda k=k, k8=k8, jj=jj, bank=bank, h=h, ob_i=ob_i:
                            nc.tensor.matmul(
                                bank[:, 0:512],
                                g_sb[:, k * TT + jj * 128: k * TT + (jj + 1) * 128],
                                w_out_sl[ob_i][:, k8 * 512:(k8 + 1) * 512],
                                start=(k == 0), stop=(h == 1 and k8 == 7))),
                            waits=(wl if k8 == 0 else []),
                            inc=("pe" if k8 == 7 else None))
                        if k8 == 7:
                            pe_last[jj] = v
                G["outslab"][ob_i] = ("pe", em.c("pe"))
                if h == 1:
                    for jj in range(NJ):
                        bname, bank = PBANKS4[jj]
                        awl = [("pe", pe_last[jj])]
                        if n2 == 0 and jj == 0 and G["r2f"] is not None:
                            awl.append(G["r2f"])
                        dve_v = em.emit("vector", (lambda bank=bank, jj=jj, n2=n2:
                            nc.vector.scalar_tensor_tensor(
                                out=r2f[:, jj * D + n2 * 512: jj * D + n2 * 512 + 512],
                                in0=bank[:, 0:512],
                                scalar=ys_tm[:, jj:jj + 1],
                                in1=bass.AP(xres, jj * D + n2 * 512,
                                            [[4 * D, 128], [1, 512]]),
                                op0=ALU.mult, op1=ALU.add)),
                            waits=awl, drain=True, inc="dve")
                        bank_set(bname, "dve", dve_v)
                        evac_out[(n2, jj)] = dve_v
        G["g"] = ("pe", em.c("pe"))
        G["xc"] = ("pe", em.c("pe"))

        res_done = {}
        for jj in range(NJ):
            res_done[jj] = evac_out[(1, jj)]
            G["xres"][jj] = ("dve", evac_out[(1, jj)])
        st["res_done"] = res_done

    def prep_xres(i, jj):
        base, first, last = tile_base(i)
        wl = [G["xres"][jj]] if G["xres"][jj] is not None else []
        def mk_xr(base=base, jj=jj):
            srcp = bass.AP(x_h.tensor, (base + jj * 128) * D, [[D, 128], [1, D]])
            dst = bass.AP(xres, jj * D, [[4 * D, 128], [1, D]])
            return nc.sync.dma_start(out=dst, in_=srcp)
        xr_tok = em.dma("sync", mk_xr, stream="xr", waits=wl)
        wl2 = [xr_tok]
        if i == 0 and jj == 0:
            wl2.append(prep_outb_done)
        em.emit("vector", (lambda jj=jj: nc.vector.scalar_tensor_tensor(
            out=bass.AP(xres, jj * D, [[4 * D, 128], [1, D]]),
            in0=bass.AP(xres, jj * D, [[4 * D, 128], [1, D]]),
            scalar=2.0, in1=outb_bc[:, :],
            op0=ALU.mult, op1=ALU.add)),
            waits=wl2, drain=True, inc="dve")

    def emit_ln(src_fn, waits):
        em.emit("vector", lambda src_fn=src_fn: nc.vector.bn_stats(
            stat[:, 0:6], src_fn()[:, 0:512]), waits=waits, drain=True)
        em.emit("vector", lambda src_fn=src_fn: nc.vector.bn_stats(
            stat[:, 6:12], src_fn()[:, 512:1024]), drain=True)
        em.emit("vector", lambda: nc.vector.bn_aggr(
            stat[:, 12:14], stat[:, 0:12]), drain=True)
        em.emit("vector", lambda: nc.vector.tensor_scalar(
            out=stat[:, 14:15], in0=stat[:, 12:13], scalar1=-1.0,
            scalar2=None, op0=ALU.mult), drain=True)
        em.emit("vector", lambda: nc.vector.tensor_scalar(
            out=stat[:, 2:3], in0=stat[:, 13:14], scalar1=1e-5, scalar2=None,
            op0=ALU.add), drain=True)
        dve_v = em.emit("vector", lambda: nc.vector.reciprocal(
            stat[:, 3:4], stat[:, 2:3]), drain=True, inc="dve")
        act_v = em.emit("scalar", lambda: nc.scalar.activation(
            out=stat[:, 6:7], in_=stat[:, 3:4], func=AF.Sqrt),
            waits=[("dve", dve_v)], drain=True, inc="act")
        return act_v

    def emit_tail_preA(i, jj):
        st = TS_[i]
        res_done = st["res_done"]
        wl = [("dve", res_done[jj])]
        if G["ln1"] is not None and jj == 0:
            wl.append(G["ln1"])
        ln_act = emit_ln(lambda jj=jj: r2f[:, jj * D:(jj + 1) * D], wl)
        ln1_done = em.emit("vector", (lambda jj=jj: nc.vector.tensor_scalar(
            out=ln1[:, jj * D:(jj + 1) * D], in0=r2f[:, jj * D:(jj + 1) * D],
            scalar1=stat[:, 14:15], scalar2=stat[:, 6:7],
            op0=ALU.add, op1=ALU.mult)),
            waits=[("act", ln_act)], drain=True, inc="dve")
        st.setdefault("ln1_done", {})[jj] = ln1_done
        if jj == NJ - 1:
            G["r2f"] = ("dve", ln1_done)

    def emit_tail_preB(i, jj):
        st = TS_[i]
        base, first, last = tile_base(i)
        ln1_done = st["ln1_done"][jj]
        if True:
            for half in range(2):
                bank, bname = (PT, "PT") if half == 0 else (PS1, "PS1")
                wl0 = [("dve", ln1_done)] + bank_wait(bname)
                pe_v = None
                for c4 in range(4):
                    c = half * 4 + c4
                    pe_v = em.emit("tensor", (lambda jj=jj, c=c, c4=c4, bank=bank:
                        nc.tensor.transpose(
                            bank[:, c4 * 128:(c4 + 1) * 128],
                            ln1[:, jj * D + c * 128: jj * D + (c + 1) * 128],
                            ident_f32[:, :])),
                        waits=(wl0 if c4 == 0 else []),
                        inc=("pe" if c4 == 3 else None))
                dve_v = em.emit("vector", (lambda half=half, bank=bank:
                    nc.vector.tensor_scalar(
                        out=qA[:, half * 512:(half + 1) * 512],
                        in0=bank[:, 0:512], scalar1=r_bc[:, 0:1], scalar2=448.0,
                        op0=ALU.mult, op1=ALU.min)),
                    waits=[("pe", pe_v)], drain=True, inc="dve")
                bank_set(bname, "dve", dve_v)

            qwl = []
            if G["xq"][jj % 2] is not None:
                qwl.append(G["xq"][jj % 2])
            xq_ready = em.emit("vector", (lambda jj=jj: nc.vector.tensor_scalar(
                out=xqs[jj % 2], in0=qA, scalar1=-448.0, scalar2=None,
                op0=ALU.max)), waits=qwl, drain=True, inc="dve")
            st.setdefault("xq_ready", {})[jj] = xq_ready

    def emit_tail_ffn(i, jj):
        st = TS_[i]
        base, first, last = tile_base(i)
        xq_ready = st["xq_ready"][jj]
        if True:
            ffn_evac = {}
            for n2 in range(2):
                bank, bname = (P4, "P4") if n2 == 0 else (P5, "P5")
                wl = [("dve", xq_ready)] + bank_wait(bname)
                if first_use["ffn"]:
                    wl.extend(toks_ffn)
                    first_use["ffn"] = False
                for k in range(NKD):
                    em.emit("tensor", (lambda k=k, n2=n2, bank=bank, jj=jj:
                        nc.tensor.matmul(
                            bank[:, 0:512],
                            xqs[jj % 2][:, k * 128:(k + 1) * 128],
                            w_ffn_res[:, k * 1024 + n2 * 512: k * 1024 + n2 * 512 + 512],
                            start=(k == 0), stop=False)),
                        waits=(wl if k == 0 else []))
                pe_v = em.emit("tensor", (lambda n2=n2, bank=bank:
                    nc.tensor.matmul(
                        bank[:, 0:512], ones_row_bf[0:1, :],
                        bp_row_bf[0:1, n2 * 512:(n2 + 1) * 512],
                        start=False, stop=True)),
                    inc="pe")
                ob = outb[jj % 2]
                awl = [("pe", pe_v)]
                if G["outb"][jj % 2] is not None and n2 == 0:
                    awl.append(G["outb"][jj % 2])
                act_v = em.emit("scalar", (lambda n2=n2, bank=bank, ob=ob:
                    nc.scalar.activation(
                        out=ob[:, n2 * 512:(n2 + 1) * 512], in_=bank[:, 0:512],
                        func=AF.Identity, scale=sp_bc[:, 0:1])),
                    waits=awl, drain=True, inc="act")
                bank_set(bname, "act", act_v)
                ffn_evac[n2] = act_v
            G["xq"][jj % 2] = ("pe", em.c("pe"))

            ob = outb[jj % 2]
            r4_done = em.emit("vector", (lambda jj=jj, ob=ob:
                nc.vector.tensor_tensor(
                    ob[:, :], ob[:, :], ln1[:, jj * D:(jj + 1) * D], ALU.add)),
                waits=[("act", max(ffn_evac[0], ffn_evac[1]))],
                drain=True, inc="dve")
            ln_act = emit_ln(lambda ob=ob: ob[:, :], [("dve", r4_done)])
            out_done = em.emit("vector", (lambda ob=ob: nc.vector.tensor_scalar(
                out=ob[:, :], in0=ob[:, :],
                scalar1=stat[:, 14:15], scalar2=stat[:, 6:7],
                op0=ALU.add, op1=ALU.mult)),
                waits=[("act", ln_act)], drain=True, inc="dve")

            def mk_out(base=base, jj=jj, ob=ob):
                dst = bass.AP(y_h.tensor, (base + jj * 128) * D, [[D, 128], [1, D]])
                return nc.gpsimd.dma_start(out=dst, in_=ob[:, :])
            st_v = em.dma("gpsimd", mk_out, stream="st", waits=[("dve", out_done)])
            G["outb"][jj % 2] = st_v
            G["ln1"] = ("dve", out_done)

    import os as _os
    _limit = int(_os.environ.get("MK_LIMIT_TILES", "0")) or (NSEQ * TPS)
    emit_front(0)
    for g0 in range(6):
        issue_gate_pair(0, g0)
    for p in range(NJ):
        emit_gate_part(0, p)
        emit_conv_part(0, p)
    for i in range(_limit):
        emit_mixer(i)
        if i + 1 < _limit:
            emit_front(i + 1)
        emit_ys(i)
        emit_out(i)
        if i + 1 < _limit:
            for g0 in range(6):
                issue_gate_pair(i + 1, g0)
        for jj in range(NJ):
            emit_tail_preA(i, jj)
            if i + 1 < _limit:
                emit_gate_part(i + 1, jj)
            emit_tail_preB(i, jj)
            if i + 1 < _limit:
                emit_conv_part(i + 1, jj)
            if jj > 0:
                emit_tail_ffn(i, jj - 1)
        emit_tail_ffn(i, NJ - 1)

    em.emit("gpsimd", lambda: None, waits=em.stream_total("st"))

    sem_stack = contextlib.ExitStack()
    sems = {name: sem_stack.enter_context(nc.semaphore(name)) for name in em.counts}

    with nc.Block() as block:
        def run_ops(eng_obj, eng_name):
            for fn, waits, inc, inc_n, drain in em.ops[eng_name]:
                if drain:
                    eng_obj.drain()
                for (sname, val) in waits:
                    if val:
                        eng_obj.wait_ge(sems[sname], val)
                inst = fn()
                if inc is not None:
                    if inst is None:
                        inst = eng_obj.drain()
                    inst.then_inc(sems[inc], inc_n)

        @block.sync
        def _(sync):
            run_ops(sync, "sync")

        @block.tensor
        def _(tensor):
            run_ops(tensor, "tensor")

        @block.scalar
        def _(scalar):
            run_ops(scalar, "scalar")

        @block.vector
        def _(vector):
            run_ops(vector, "vector")

        @block.gpsimd
        def _(gpsimd):
            run_ops(gpsimd, "gpsimd")

    es.close()
    sem_stack.close()
    return nc


_NC_CACHE = None


def kernel(**inputs):
    global _NC_CACHE
    import ml_dtypes

    f32 = np.float32
    cc = np.ascontiguousarray
    x = cc(np.asarray(inputs["L0_attn_x"], dtype=f32))
    in_w = np.asarray(inputs["L0_attn_in_w"], dtype=f32)
    in_b = np.asarray(inputs["L0_attn_in_b"], dtype=f32)
    conv_w = np.asarray(inputs["L0_attn_conv_w"], dtype=f32)
    conv_b = np.asarray(inputs["L0_attn_conv_b"], dtype=f32)
    dt_w = np.asarray(inputs["L0_attn_dt_w"], dtype=f32)
    dt_b = np.asarray(inputs["L0_attn_dt_b"], dtype=f32)
    B_w = np.asarray(inputs["L0_attn_B_w"], dtype=f32)
    B_b = np.asarray(inputs["L0_attn_B_b"], dtype=f32)
    C_w = np.asarray(inputs["L0_attn_C_w"], dtype=f32)
    C_b = np.asarray(inputs["L0_attn_C_b"], dtype=f32)
    D_w = np.asarray(inputs["L0_attn_D_w"], dtype=f32)
    D_b = np.asarray(inputs["L0_attn_D_b"], dtype=f32)
    out_w = np.asarray(inputs["L0_attn_out_w"], dtype=f32)
    out_b = np.asarray(inputs["L0_attn_out_b"], dtype=f32)
    ffn_w = np.asarray(inputs["L0_ffn_weight_fp8"], dtype=f32)
    ffn_b = np.asarray(inputs["L0_ffn_bias"], dtype=f32)
    in_scale = np.asarray(inputs["L0_ffn_input_scale"], dtype=f32)
    w_scale = np.asarray(inputs["L0_ffn_weight_scale"], dtype=f32)

    common = {
        "in_wT": cc(in_w.T),
        "dt_wT": cc(dt_w.T),
        "out_wT": cc(out_w.T),
        "ffn_wT": cc(ffn_w.T),
        "bcd_wT": cc(np.concatenate([B_w.T, C_w.T, D_w.T], axis=1)),
        "in_b_pm": cc(in_b.reshape(32, 128).T),
        "dt_b_pm": cc(dt_b.reshape(16, 128).T),
        "conv_b_pm": cc(conv_b.reshape(16, 128).T),
        "conv_w_pm": cc(conv_w[:, 0, :].reshape(16, 128, 3).transpose(1, 0, 2).reshape(128, 48)),
        "bcd_b3": cc(np.stack([B_b, C_b, D_b], axis=1)),
        "out_b_row": cc(out_b[None, :]),
        "ffn_b_row": cc(ffn_b[None, :]),
        "in_scale": cc(in_scale.reshape(1, 1)),
        "w_scale": cc(w_scale.reshape(1, 1)),
        "ident_f32": np.eye(128, dtype=f32),
        "ident_bf": np.eye(128, dtype=ml_dtypes.bfloat16),
        "ones_row_f32": np.ones((1, 128), dtype=f32),
        "ones_row_bf": np.ones((1, 128), dtype=ml_dtypes.bfloat16),
        "ones_col_bf": np.ones((128, 1), dtype=ml_dtypes.bfloat16),
    }
    n_cores = 8
    xr = x.reshape(32, TSEQ, D)
    in_maps = []
    for c in range(n_cores):
        m = dict(common)
        m["x"] = cc(xr[c * NSEQ:(c + 1) * NSEQ].reshape(NTOK, D))
        in_maps.append(m)

    if _NC_CACHE is None:
        _NC_CACHE = build_nc()
    res = run_bass_kernel_spmd(_NC_CACHE, in_maps, core_ids=list(range(n_cores)))
    out = np.concatenate(
        [res.results[c]["y"].reshape(NSEQ, TSEQ, D) for c in range(n_cores)], axis=0)
    return np.ascontiguousarray(out, dtype=np.float32)


# version 21 (final)
# speedup vs baseline: 1.0673x; 1.0324x over previous
import contextlib
import numpy as np
import concourse.bass as bass
import concourse.mybir as mybir
from concourse.bass_utils import run_bass_kernel_spmd

F32 = mybir.dt.float32
BF16 = mybir.dt.bfloat16
I32 = mybir.dt.int32
AF = mybir.ActivationFunctionType
ALU = mybir.AluOpType

D = 1024
E = 2048
TSEQ = 2048
NSEQ = 4
TT = 512
TPS = TSEQ // TT
NJ = TT // 128
NKD = D // 128
NKE = E // 128
NTOK = NSEQ * TSEQ

ENGINES = ("sync", "tensor", "scalar", "vector", "gpsimd")


DMA_ROT = 8


class Emitter:
    def __init__(self):
        self.ops = {e: [] for e in ENGINES}
        self.counts = {}
        self.stream_idx = {}

    def sem(self, name):
        self.counts.setdefault(name, 0)

    def emit(self, eng, fn, waits=(), inc=None, inc_n=1, drain=False):
        val = None
        if inc is not None:
            self.counts[inc] = self.counts.get(inc, 0) + inc_n
            val = self.counts[inc]
        self.ops[eng].append((fn, tuple(waits), inc, inc_n, drain))
        return val

    def dma(self, eng, fn, stream, waits=()):
        idx = self.stream_idx.get(stream, 0)
        self.stream_idx[stream] = idx + 1
        sem = f"{stream}{idx % DMA_ROT}"
        self.sem(sem)
        w = list(waits)
        prev = 16 * (idx // DMA_ROT)
        if prev > 0:
            w.append((sem, prev))
        self.counts[sem] = self.counts.get(sem, 0) + 16
        self.ops[eng].append((fn, tuple(w), sem, 16, False))
        return (sem, 16 * (idx // DMA_ROT + 1))

    def stream_total(self, stream):
        n = self.stream_idx.get(stream, 0)
        toks = []
        for r in range(DMA_ROT):
            cnt = (n - r + DMA_ROT - 1) // DMA_ROT
            if cnt > 0:
                toks.append((f"{stream}{r}", 16 * cnt))
        return toks

    def c(self, name):
        return self.counts.get(name, 0)


def build_nc():
    nc = bass.Bass("TRN2", target_bir_lowering=False)

    def din(name, shape, dt=F32):
        return nc.dram_tensor(name, shape, dt, kind="ExternalInput").ap()

    x_h = din("x", [NTOK, D])
    w_in_h = din("in_wT", [D, 2 * E])
    w_dt_h = din("dt_wT", [E, E])
    w_out_h = din("out_wT", [E, D])
    w_ffn_h = din("ffn_wT", [D, D])
    w_bcd_h = din("bcd_wT", [E, 48])
    in_b_pm_h = din("in_b_pm", [128, 32])
    dt_b_pm_h = din("dt_b_pm", [128, 16])
    conv_b_pm_h = din("conv_b_pm", [128, 16])
    conv_w_pm_h = din("conv_w_pm", [128, 48])
    bcd_b3_h = din("bcd_b3", [16, 3])
    out_b_row_h = din("out_b_row", [1, D])
    ffn_b_row_h = din("ffn_b_row", [1, D])
    in_scale_h = din("in_scale", [1, 1])
    w_scale_h = din("w_scale", [1, 1])
    ident_f32_h = din("ident_f32", [128, 128])
    ident_bf_h = din("ident_bf", [128, 128], BF16)
    ones_row_f32_h = din("ones_row_f32", [1, 128])
    ones_row_bf_h = din("ones_row_bf", [1, 128], BF16)
    ones_col_bf_h = din("ones_col_bf", [128, 1], BF16)

    y_h = nc.dram_tensor("y", [NTOK, D], F32, kind="ExternalOutput").ap()

    sc_in = nc.dram_tensor("sc_in", [32 * 128 * 1024], BF16, kind="Internal").ap()
    sc_dt = nc.dram_tensor("sc_dt", [16 * 128 * 2048], BF16, kind="Internal").ap()
    sc_out = nc.dram_tensor("sc_out", [2 * 128 * 8192], BF16, kind="Internal").ap()

    em = Emitter()
    for s in ("pe", "act", "dve", "gp"):
        em.sem(s)

    es = contextlib.ExitStack()

    def sb(name, shape, dt=F32):
        return es.enter_context(nc.sbuf_tensor(name, shape, dt))

    def psumt(name, shape, dt=F32):
        return es.enter_context(nc.psum_tensor(name, shape, dt))

    xA = sb("xA", [128, 2 * D], BF16)
    xres = sb("xres", [128, 4 * D])
    x_fm = sb("x_fm", [128, NKD * (TT + 1)], BF16)
    xm = sb("xm", [128, NKE * (TT + 2)], BF16)
    xc = sb("xc", [128, NKE * TT], BF16)
    g_sb = sb("g_sb", [128, NKE * TT], BF16)
    cbuf = [[sb(f"cb{s}{t}", [128, TT], BF16) for t in range(2)] for s in range(2)]
    cvt = sb("cvt", [128, TT])
    sp_bf = [sb("sp0", [128, TT], BF16), sb("sp1", [128, TT], BF16)]
    w_in_sl = [sb(f"win{t}", [128, 2048], BF16) for t in range(4)]
    w_dt_sl = [sb(f"wdt{t}", [128, 2048], BF16) for t in range(6)]
    w_out_sl = [sb(f"wout{t}", [128, 4096], BF16) for t in range(2)]
    w_ffn_res = sb("wffn", [128, 8 * 1024], mybir.dt.float8e4)
    w_bcd_res = sb("wbcd", [128, 16 * 48], BF16)
    decay16 = sb("decay16", [16, TT])
    u16 = sb("u16", [16, TT])
    c16 = sb("c16", [16, TT])
    cs16 = sb("cs16", [16, TT])
    du16 = cs16
    st16s = sb("st16a", [16, TT])
    state_col = sb("state_col", [16, 1])
    ys_tm = sb("ys_tm", [128, NJ])
    r2f = sb("r2f", [128, NJ * D])
    ln1 = r2f
    scrF = sb("scrF", [128, 1024])
    qA = scrF[:, 0:1024]
    outb = [sb("outb0", [128, D]), sb("outb1", [128, D])]
    stat = sb("stat", [128, 16])
    xq8 = sb("xq8", [128, 2048], mybir.dt.float8e4)
    xqs = [xq8[:, 0:1024], xq8[:, 1024:2048]]
    in_b_pm = sb("c_in_b_pm", [128, 32])
    dt_b_pm = sb("c_dt_b_pm", [128, 16])
    conv_b_pm = sb("c_conv_b_pm", [128, 16])
    conv_w_pm = sb("c_conv_w_pm", [128, 48])
    bcd_b3 = sb("c_bcd_b3", [16, 3])
    out_b_row = sb("c_out_b_row", [1, D])
    ffn_b_row = sb("c_ffn_b_row", [1, D])
    outb_bc = sb("outb_bc", [128, D])
    bp_row_bf = sb("bp_row_bf", [1, D], BF16)
    sc_sc = sb("sc_sc", [1, 8])
    r_bc = sb("r_bc", [128, 1])
    sp_bc = sb("sp_bc", [128, 1])
    ident_f32 = sb("c_ident_f32", [128, 128])
    ident_bf = sb("c_ident_bf", [128, 128], BF16)
    ones_row_f32 = sb("c_ones_row_f32", [1, 128])
    ones_row_bf = sb("c_ones_row_bf", [1, 128], BF16)
    ones_col_bf = sb("c_ones_col_bf", [128, 1], BF16)

    xh_sb = out_b_row
    decay_t = ffn_b_row[0:1, 0:512]
    decay_row = ffn_b_row[0:1, 512:1024]

    P0 = psumt("P0", [128, 512])
    P1 = psumt("P1", [128, 512])
    P2 = psumt("P2", [128, 512])
    P3 = psumt("P3", [128, 512])
    P4 = psumt("P4", [128, 512])
    P5 = psumt("P5", [128, 512])
    PT = psumt("PT", [128, 512])
    PS1 = psumt("PS1", [128, 512])
    PMAIN = [P0, P1]
    PHALO = [P2, P3]

    bank_free = {}

    def bank_wait(bname):
        w = bank_free.pop(bname, None)
        return [w] if w is not None else []

    def bank_set(bname, sem, cnt):
        bank_free[bname] = (sem, cnt)

    const_loads = [
        (in_b_pm, in_b_pm_h), (dt_b_pm, dt_b_pm_h), (conv_b_pm, conv_b_pm_h),
        (conv_w_pm, conv_w_pm_h), (bcd_b3, bcd_b3_h), (out_b_row, out_b_row_h),
        (ffn_b_row, ffn_b_row_h), (ident_f32, ident_f32_h),
        (ident_bf, ident_bf_h),
        (ones_row_f32, ones_row_f32_h), (ones_row_bf, ones_row_bf_h),
        (ones_col_bf, ones_col_bf_h),
    ]
    cl_tok = None
    for dst, srct in const_loads:
        cl_tok = em.dma("sync", (lambda d=dst, s=srct: nc.sync.dma_start(out=d[:, :], in_=s[:, :])),
                        stream="cl")
    cl_tok = em.dma("sync", lambda: nc.sync.dma_start(out=sc_sc[0:1, 0:1], in_=in_scale_h[:, :]), stream="cl")
    cl_tok = em.dma("sync", lambda: nc.sync.dma_start(out=sc_sc[0:1, 1:2], in_=w_scale_h[:, :]), stream="cl")
    cl_all = em.stream_total("cl")

    em.emit("vector", lambda: nc.vector.tensor_tensor(
        sc_sc[0:1, 2:3], sc_sc[0:1, 0:1], sc_sc[0:1, 1:2], ALU.mult),
        waits=cl_all, drain=True)
    em.emit("vector", lambda: nc.vector.reciprocal(sc_sc[0:1, 3:4], sc_sc[0:1, 2:3]), drain=True)
    em.emit("vector", lambda: nc.vector.reciprocal(sc_sc[0:1, 4:5], sc_sc[0:1, 0:1]), drain=True)
    em.emit("vector", lambda: nc.vector.tensor_scalar(
        out=bp_row_bf[:, :], in0=ffn_b_row[:, :], scalar1=sc_sc[0:1, 3:4],
        scalar2=None, op0=ALU.mult), drain=True)
    dve_prep = em.emit("vector", lambda: nc.vector.tensor_copy(
        sc_sc[0:1, 5:6], sc_sc[0:1, 0:1]), drain=True, inc="dve")
    em.emit("tensor", lambda: nc.tensor.matmul(
        PT[:, 0:1], ones_row_f32[0:1, :], sc_sc[0:1, 4:5], start=True, stop=True),
        waits=[("dve", dve_prep)])
    pe_bc = em.emit("tensor", lambda: nc.tensor.matmul(
        PT[:, 1:2], ones_row_f32[0:1, :], sc_sc[0:1, 2:3], start=True, stop=True),
        inc="pe")
    em.emit("vector", lambda: nc.vector.tensor_copy(r_bc[:, :], PT[:, 0:1]),
            waits=[("pe", pe_bc)], drain=True)
    dve_bc = em.emit("vector", lambda: nc.vector.tensor_copy(sp_bc[:, :], PT[:, 1:2]),
                     drain=True, inc="dve")
    pe_ob = em.emit("tensor", lambda: nc.tensor.matmul(
        PS1[:, 0:512], ones_row_f32[0:1, :], out_b_row[0:1, 0:512],
        start=True, stop=True), waits=[("dve", dve_prep)], inc="pe")
    em.emit("vector", lambda: nc.vector.tensor_copy(outb_bc[:, 0:512], PS1[:, 0:512]),
            waits=[("pe", pe_ob)], drain=True, inc="dve")
    dve_ob1 = em.c("dve")
    pe_ob2 = em.emit("tensor", lambda: nc.tensor.matmul(
        PS1[:, 0:512], ones_row_f32[0:1, :], out_b_row[0:1, 512:1024],
        start=True, stop=True), waits=[("dve", dve_ob1)], inc="pe")
    dve_ob2 = em.emit("vector", lambda: nc.vector.tensor_copy(
        outb_bc[:, 512:1024], PS1[:, 0:512]),
        waits=[("pe", pe_ob2)], drain=True, inc="dve")
    bank_set("PT", "dve", dve_bc)
    bank_set("PS1", "dve", dve_ob2)
    gate_outb_row = ("pe", pe_ob2)
    prep_outb_done = ("dve", dve_ob2)

    X0_TOKS = {}
    for jx in range(2):
        toks = []
        for hh in range(2):
            def mk_x0(jx=jx, hh=hh):
                srcp = bass.AP(x_h.tensor, (jx * 128) * D + hh * 512,
                               [[D, 128], [1, 512]])
                dst = bass.AP(xA, (jx % 2) * D + hh * 512, [[2 * D, 128], [1, 512]])
                return nc.gpsimd.dma_start(out=dst, in_=srcp)
            toks.append(em.dma("gpsimd", mk_x0, stream="x"))
        X0_TOKS[jx] = toks

    toks_in = []
    for k in range(NKD):
        for ch in range(2):
            t = em.dma("gpsimd", (lambda k=k, ch=ch: nc.gpsimd.dma_start(
                out=bass.AP(sc_in.tensor, k * 128 + ch * 1024,
                            [[2048, 128], [128 * 2048, 16], [1, 128]]),
                in_=bass.AP(w_in_h.tensor, k * 128 * 4096 + ch * 128,
                            [[4096, 128], [256, 16], [1, 128]]))),
                stream="pr")
            toks_in.append(t)
    toks_dt = []
    for k in range(NKE):
        t = em.dma("gpsimd", (lambda k=k: nc.gpsimd.dma_start(
            out=bass.AP(sc_dt.tensor, k * 128, [[2048, 128], [128 * 2048, 16], [1, 128]]),
            in_=bass.AP(w_dt_h.tensor, k * 128 * 2048, [[2048, 128], [128, 16], [1, 128]]))),
            stream="pr")
        toks_dt.append(t)
    toks_out = []
    for k in range(NKE):
        t = em.dma("gpsimd", (lambda k=k: nc.gpsimd.dma_start(
            out=bass.AP(sc_out.tensor, k * 512, [[8192, 128], [128 * 8192, 2], [1, 512]]),
            in_=bass.AP(w_out_h.tensor, k * 128 * 1024, [[1024, 128], [512, 2], [1, 512]]))),
            stream="pr")
        toks_out.append(t)
    toks_ffn = []
    ffn_stage_done = [None, None]
    for k in range(NKD):
        stg = w_out_sl[k % 2]
        wl = []
        if ffn_stage_done[k % 2] is not None:
            wl.append(("dve", ffn_stage_done[k % 2]))
        t = em.dma("gpsimd", (lambda k=k, stg=stg: nc.gpsimd.dma_start(
            out=stg[:, 0:1024],
            in_=w_ffn_h[k * 128:(k + 1) * 128, :])),
            stream="pr", waits=wl)
        dve_c = em.emit("vector", (lambda k=k, stg=stg: nc.vector.tensor_copy(
            w_ffn_res[:, k * 1024:(k + 1) * 1024], stg[:, 0:1024])),
            waits=[t], drain=True, inc="dve")
        ffn_stage_done[k % 2] = dve_c
        toks_ffn.append(("dve", dve_c))
    ffn_stage_toks = [("dve", ffn_stage_done[0]), ("dve", ffn_stage_done[1])]
    toks_bcd = []
    for k in range(NKE):
        t = em.dma("gpsimd", (lambda k=k: nc.gpsimd.dma_start(
            out=w_bcd_res[:, k * 48:(k + 1) * 48],
            in_=w_bcd_h[k * 128:(k + 1) * 128, :])),
            stream="pr")
        toks_bcd.append(t)
    toks_in = toks_in[-DMA_ROT:]
    toks_dt = toks_dt[-DMA_ROT:]
    toks_out = toks_out[-DMA_ROT:]
    toks_ffn = toks_ffn[-DMA_ROT:]
    toks_bcd = toks_bcd[-DMA_ROT:]

    G = {
        "x_slot": [None, None],
        "xres": [None] * 4,
        "xfm": None,
        "inslab": [None] * 4,
        "dtslab": [None] * 6,
        "slab_ring": [0],
        "outslab": ffn_stage_toks[:],
        "sp": [None, None],
        "cbuf": [None, None],
        "outb": [None, None],
        "xq": [None, None],
        "r2f": None,
        "ln1": None,
        "g": None,
        "xc": None,
        "xm_taps": None,
        "cs": None,
        "scan": None,
    }
    first_use = {"in": True, "dt": True, "out": True, "ffn": True, "bcd": True}
    TS_ = {}

    def tile_base(i):
        s, ti = divmod(i, TPS)
        return s * TSEQ + ti * TT, (ti == 0), (ti == TPS - 1)

    def emit_front(i):
        base, first, last = tile_base(i)
        st = TS_.setdefault(i, {})
        x_toks = {}

        def load_x(jx):
            if i == 0 and jx < 2:
                x_toks[jx] = X0_TOKS[jx]
                return
            slot = (i * NJ + jx) % 2
            wl = [G["x_slot"][slot]] if G["x_slot"][slot] is not None else []
            toks = []
            for hh in range(2):
                def mk_x(base=base, jx=jx, slot=slot, hh=hh):
                    srcp = bass.AP(x_h.tensor, (base + jx * 128) * D + hh * 512,
                                   [[D, 128], [1, 512]])
                    dst = bass.AP(xA, slot * D + hh * 512, [[2 * D, 128], [1, 512]])
                    return nc.gpsimd.dma_start(out=dst, in_=srcp)
                toks.append(em.dma("gpsimd", mk_x, stream="x",
                                   waits=(wl if hh == 0 else [])))
            x_toks[jx] = toks
        load_x(0)
        load_x(1)
        if not last:
            def mk_xh(base=base):
                return nc.sync.dma_start(
                    out=xh_sb[0:1, :], in_=x_h[base + TT:base + TT + 1, :])
            st["xh_tok"] = em.dma("sync", mk_xh, stream="xh",
                                  waits=([gate_outb_row] if i == 0 else []))
        BANKS4 = [P0, P1, P2, P3]
        for jj in range(NJ):
            slot = (i * NJ + jj) % 2
            last_pe = None
            for c4 in range(2):
                bidx = (jj * 2 + c4) % 4
                bank, bname = BANKS4[bidx], f"P{bidx}"
                wl = [x_toks[jj][c4]] + bank_wait(bname)
                pe_v = None
                for q in range(4):
                    c = c4 * 4 + q
                    pe_v = em.emit("tensor", (lambda c=c, q=q, bank=bank, slot=slot:
                        nc.tensor.transpose(
                            bank[:, q * 64:(q + 1) * 64].bitcast(BF16),
                            bass.AP(xA, slot * D + c * 128, [[2 * D, 128], [1, 128]]),
                            ident_bf[:, :])),
                        waits=(wl if q == 0 else []), inc=("pe" if q == 3 else None))
                awl = [("pe", pe_v)]
                if jj == 0 and c4 == 0 and G["xfm"] is not None:
                    awl.append(G["xfm"])
                act_v = em.emit("scalar", (lambda jj=jj, c4=c4, bank=bank:
                    nc.scalar.copy(
                        bass.AP(x_fm, c4 * 4 * (TT + 1) + jj * 128,
                                [[NKD * (TT + 1), 128], [TT + 1, 4], [1, 128]]),
                        bank[:, 0:256].bitcast(BF16))),
                    waits=awl, drain=True, inc="act")
                bank_set(bname, "act", act_v)
                last_pe = pe_v
            G["x_slot"][slot] = ("pe", last_pe)
            if jj + 2 < NJ:
                load_x(jj + 2)
        if not last:
            for c4 in range(2):
                bank, bname = BANKS4[c4], f"P{c4}"
                wl = [st["xh_tok"]] + bank_wait(bname)
                pe_v = None
                for q in range(4):
                    c = c4 * 4 + q
                    pe_v = em.emit("tensor", (lambda c=c, q=q, bank=bank:
                        nc.tensor.transpose(
                            bank[:, q:q + 1],
                            xh_sb[0:1, c * 128:(c + 1) * 128],
                            ident_f32[0:1, 0:1])),
                        waits=(wl if q == 0 else []), inc=("pe" if q == 3 else None))
                act_v = em.emit("scalar", (lambda c4=c4, bank=bank:
                    nc.scalar.copy(
                        bass.AP(x_fm, c4 * 4 * (TT + 1) + TT,
                                [[NKD * (TT + 1), 128], [TT + 1, 4], [1, 1]]),
                        bank[:, 0:4])),
                    waits=[("pe", pe_v)], drain=True, inc="act")
                bank_set(bname, "act", act_v)
        st["xfm_ready"] = em.c("act")

        wl = [G["xm_taps"]] if G["xm_taps"] is not None else []
        if first:
            def mk_hl():
                return nc.vector.memset(
                    bass.AP(xm, 0, [[NKE * (TT + 2), 128], [TT + 2, NKE], [1, 1]]), 0.0)
        else:
            def mk_hl():
                srcp = bass.AP(xm, TT, [[NKE * (TT + 2), 128], [TT + 2, NKE], [1, 1]])
                dst = bass.AP(xm, 0, [[NKE * (TT + 2), 128], [TT + 2, NKE], [1, 1]])
                return nc.vector.tensor_copy(dst, srcp)
        dve_hl = em.emit("vector", mk_hl, waits=wl, drain=True, inc="dve")
        if last:
            dve_hl = em.emit("vector", lambda: nc.vector.memset(
                bass.AP(xm, TT + 1, [[NKE * (TT + 2), 128], [TT + 2, NKE], [1, 1]]), 0.0),
                drain=True, inc="dve")
        st["dve_hl"] = dve_hl

        st["inproj_act"] = {}
        for mp in range(8):
            b = mp % 4
            wl = [G["inslab"][b]] if G["inslab"][b] is not None else []
            if first_use["in"]:
                wl.extend(toks_in)
                first_use["in"] = False
            def mk_sl(mp=mp, b=b):
                srcp = bass.AP(sc_in.tensor, mp * 128 * 2048, [[2048, 128], [1, 2048]])
                dstp = bass.AP(w_in_sl[b], 0, [[2048, 128], [1, 2048]])
                return nc.sync.dma_start(out=dstp, in_=srcp)
            sl_v = em.dma("sync", mk_sl, stream="wi", waits=wl)
            pe_v = None
            for half in range(2):
                m = 2 * mp + half
                bname = "P0" if m % 2 == 0 else "P1"
                wl = [sl_v, ("act", st["xfm_ready"])] + bank_wait(bname)
                for k in range(NKD):
                    pe_v = em.emit("tensor", (lambda k=k, b=b, half=half, m=m:
                        nc.tensor.matmul(
                            PMAIN[m % 2][:, 0:512],
                            w_in_sl[b][:, half * 1024 + k * 128: half * 1024 + (k + 1) * 128],
                            x_fm[:, k * (TT + 1): k * (TT + 1) + TT],
                            start=(k == 0), stop=(k == NKD - 1))),
                        waits=(wl if k == 0 else []), inc=("pe" if k == NKD - 1 else None))
                act_v = em.emit("scalar", (lambda m=m:
                    nc.scalar.activation(
                        out=xm[:, m * (TT + 2) + 1: m * (TT + 2) + 1 + TT],
                        in_=PMAIN[m % 2][:, 0:512], func=AF.Identity,
                        bias=in_b_pm[:, m:m + 1], scale=1.0)),
                    waits=[("pe", pe_v), ("dve", dve_hl)], drain=True, inc="act")
                bank_set(bname, "act", act_v)
                st["inproj_act"][m] = act_v
                if not last:
                    hname = "P2" if m % 2 == 0 else "P3"
                    hbank = PHALO[m % 2]
                    hwl = bank_wait(hname)
                    for k in range(NKD):
                        pe_v = em.emit("tensor", (lambda k=k, b=b, half=half, hbank=hbank:
                            nc.tensor.matmul(
                                hbank[:, 0:1],
                                w_in_sl[b][:, half * 1024 + k * 128: half * 1024 + (k + 1) * 128],
                                x_fm[:, k * (TT + 1) + TT: k * (TT + 1) + TT + 1],
                                start=(k == 0), stop=(k == NKD - 1))),
                            waits=(hwl if k == 0 else []),
                            inc=("pe" if k == NKD - 1 else None))
                    act_v2 = em.emit("scalar", (lambda m=m, hbank=hbank:
                        nc.scalar.activation(
                            out=xm[:, m * (TT + 2) + TT + 1: m * (TT + 2) + TT + 2],
                            in_=hbank[:, 0:1], func=AF.Identity,
                            bias=in_b_pm[:, m:m + 1], scale=1.0)),
                        waits=[("pe", pe_v)], drain=True, inc="act")
                    bank_set(hname, "act", act_v2)
                    st["inproj_act"][m] = act_v2
            G["inslab"][b] = ("pe", pe_v)

    def ring_next():
        r = G["slab_ring"][0]
        G["slab_ring"][0] = (r + 1) % 6
        return r

    def issue_gate_pair(i, g):
        st = TS_[i]
        m0 = 16 + 2 * g
        b = ring_next()
        wl = [G["dtslab"][b]] if G["dtslab"][b] is not None else []
        def mk_sl(g=g, b=b):
            srcp = bass.AP(sc_in.tensor, (8 + g) * 128 * 2048,
                           [[2048, 128], [1, 2048]])
            dstp = bass.AP(w_dt_sl[b], 0, [[2048, 128], [1, 2048]])
            return nc.sync.dma_start(out=dstp, in_=srcp)
        st.setdefault("gate_sl", {})[g] = (
            em.dma("sync", mk_sl, stream="wg", waits=wl), b)
        G["dtslab"][b] = None

    def emit_gate_part(i, part):
        base, first, last = tile_base(i)
        st = TS_[i]
        for pair in range(2):
            g = 2 * part + pair
            sl_v, b = st["gate_sl"][g]
            pe_v = None
            for half in range(2):
                mg = 4 * part + 2 * pair + half
                gbank = [P0, P1, P2, P3][mg % 4]
                bname = f"P{mg % 4}"
                wl = [sl_v, ("act", st["xfm_ready"])] + bank_wait(bname)
                for k in range(NKD):
                    pe_v = em.emit("tensor", (lambda k=k, b=b, half=half, gbank=gbank:
                        nc.tensor.matmul(
                            gbank[:, 0:512],
                            w_dt_sl[b][:, half * 1024 + k * 128: half * 1024 + (k + 1) * 128],
                            x_fm[:, k * (TT + 1): k * (TT + 1) + TT],
                            start=(k == 0), stop=(k == NKD - 1))),
                        waits=(wl if k == 0 else []), inc=("pe" if k == NKD - 1 else None))
                gwl = [("pe", pe_v)]
                if mg == 0 and G["g"] is not None:
                    gwl.append(G["g"])
                act_v = em.emit("scalar", (lambda mg=mg, gbank=gbank:
                    nc.scalar.activation(
                        out=g_sb[:, mg * TT:(mg + 1) * TT],
                        in_=gbank[:, 0:512], func=AF.Silu,
                        bias=in_b_pm[:, 16 + mg:16 + mg + 1], scale=1.0)),
                    waits=gwl, drain=True, inc="act")
                bank_set(bname, "act", act_v)
            G["dtslab"][b] = ("pe", pe_v)
            if g + 6 < 8:
                issue_gate_pair(i, g + 6)
        if part == 3:
            G["xfm"] = ("pe", em.c("pe"))

    def emit_conv_part(i, part):
        st = TS_[i]
        if part == 0:
            st.setdefault("conv_act", {})
        for m in range(4 * part, 4 * part + 4):
            cb = cbuf[m % 2]
            base_c = m * (TT + 2)
            wl0 = [("act", st["inproj_act"][m])]
            if G["cbuf"][m % 2] is not None:
                wl0.append(G["cbuf"][m % 2])
            em.emit("vector", (lambda m=m, cb=cb, base_c=base_c:
                nc.vector.tensor_scalar(
                    out=cb[0][:, :], in0=xm[:, base_c: base_c + TT],
                    scalar1=conv_w_pm[:, 3 * m: 3 * m + 1], scalar2=None,
                    op0=ALU.mult)),
                waits=wl0, drain=True)
            em.emit("vector", (lambda m=m, cb=cb, base_c=base_c:
                nc.vector.scalar_tensor_tensor(
                    out=cb[0][:, :], in0=xm[:, base_c + 1: base_c + 1 + TT],
                    scalar=conv_w_pm[:, 3 * m + 1: 3 * m + 2], in1=cb[0][:, :],
                    op0=ALU.mult, op1=ALU.add)),
                drain=True)
            dve_v = em.emit("vector", (lambda m=m, cb=cb, base_c=base_c:
                nc.vector.scalar_tensor_tensor(
                    out=cb[0][:, :], in0=xm[:, base_c + 2: base_c + 2 + TT],
                    scalar=conv_w_pm[:, 3 * m + 2: 3 * m + 3], in1=cb[0][:, :],
                    op0=ALU.mult, op1=ALU.add)),
                drain=True, inc="dve")
            xwl = [("dve", dve_v)]
            if m == 0 and G["xc"] is not None:
                xwl.append(G["xc"])
            act_v = em.emit("scalar", (lambda m=m, cb=cb:
                nc.scalar.activation(
                    out=xc[:, m * TT:(m + 1) * TT], in_=cb[0][:, :], func=AF.Silu,
                    bias=conv_b_pm[:, m:m + 1], scale=1.0)),
                waits=xwl, drain=True, inc="act")
            st["conv_act"][m] = act_v
            G["cbuf"][m % 2] = ("act", act_v)
        if part == 3:
            st["xc_ready"] = em.c("act")
            G["xm_taps"] = ("dve", em.c("dve"))

    def emit_mixer(i):
        st = TS_[i]
        base, first, last = tile_base(i)
        xc_ready = st["xc_ready"]
        for jjx in range(NJ):
            prep_xres(i, jjx)
        pend_ones = None
        pe_v2 = None
        for m in range(NKE):
            b = m % 2
            sbi = ring_next()
            wl = [G["dtslab"][sbi]] if G["dtslab"][sbi] is not None else []
            if first_use["dt"]:
                wl.extend(toks_dt)
                first_use["dt"] = False
            def mk_sl(m=m, sbi=sbi):
                srcp = bass.AP(sc_dt.tensor, m * 128 * 2048, [[2048, 128], [1, 2048]])
                return nc.sync.dma_start(out=w_dt_sl[sbi][:, :], in_=srcp)
            sl_v = em.dma("sync", mk_sl, stream="wd", waits=wl)
            bname = "P0" if b == 0 else "P1"
            wl = [sl_v, ("act", xc_ready)] + bank_wait(bname)
            pe_v = None
            for k in range(NKE):
                pe_v = em.emit("tensor", (lambda k=k, b=b, sbi=sbi:
                    nc.tensor.matmul(
                        PMAIN[b][:, 0:512],
                        w_dt_sl[sbi][:, k * 128:(k + 1) * 128],
                        xc[:, k * TT:(k + 1) * TT],
                        start=(k == 0), stop=(k == NKE - 1))),
                    waits=(wl if k == 0 else []), inc=("pe" if k == NKE - 1 else None))
            G["dtslab"][sbi] = ("pe", pe_v)
            if pend_ones is not None:
                pe_v2 = em.emit("tensor", pend_ones[0], waits=pend_ones[1], inc="pe")
                G["sp"][pend_ones[2]] = ("pe", pe_v2)
            swl = [("pe", pe_v)]
            if G["sp"][b] is not None:
                swl.append(G["sp"][b])
            em.emit("scalar", (lambda b=b, m=m:
                nc.scalar.activation(
                    out=cvt[:, :], in_=PMAIN[b][:, 0:512], func=AF.Exp,
                    bias=dt_b_pm[:, m:m + 1], scale=1.0)),
                waits=swl, drain=True)
            act_v = em.emit("scalar", (lambda b=b:
                nc.scalar.activation(
                    out=sp_bf[b][:, :], in_=cvt[:, :], func=AF.Ln,
                    bias=1.0, scale=1.0)),
                drain=True, inc="act")
            bank_set(bname, "act", act_v)
            owl = [("act", act_v)]
            if m == 0:
                owl += bank_wait("PS1")
            pend_ones = ((lambda b=b, m=m: nc.tensor.matmul(
                PS1[0:1, 0:512], ones_col_bf[:, 0:1], sp_bf[b][:, :],
                start=(m == 0), stop=(m == NKE - 1))), owl, b)
        pe_v2 = em.emit("tensor", pend_ones[0], waits=pend_ones[1], inc="pe")
        G["sp"][pend_ones[2]] = ("pe", pe_v2)
        dtsum_done = pe_v2

        dve_v = em.emit("vector", lambda: nc.vector.tensor_scalar(
            out=decay_t, in0=PS1[0:1, 0:512],
            scalar1=float(-1.0 / 2048.0), scalar2=-10.0,
            op0=ALU.mult, op1=ALU.max),
            waits=[("pe", dtsum_done)], drain=True, inc="dve")
        bank_set("PS1", "dve", dve_v)
        act_v = em.emit("scalar", lambda: nc.scalar.activation(
            out=decay_row, in_=decay_t, func=AF.Exp),
            waits=[("dve", dve_v)], drain=True, inc="act")
        wl = [("act", act_v)] + bank_wait("PT")
        pe_v = em.emit("tensor", lambda: nc.tensor.matmul(
            PT[:, 0:512], ones_row_f32[0:1, :], decay_row,
            start=True, stop=True),
            waits=wl, inc="pe")
        dve_v = em.emit("vector", lambda: nc.vector.tensor_copy(
            decay16[:, :], PT[0:16, 0:512]),
            waits=[("pe", pe_v)], drain=True, inc="dve")
        bank_set("PT", "dve", dve_v)

        bcd_banks = [("P2", P2, 0), ("P3", P3, 16), ("P4", P4, 32)]
        bcd_done = {}
        for bn, bank, off in bcd_banks:
            wl = [("act", xc_ready)] + bank_wait(bn)
            if first_use["bcd"]:
                wl.extend(toks_bcd)
                first_use["bcd"] = False
            pe_v = None
            for k in range(NKE):
                pe_v = em.emit("tensor", (lambda k=k, bank=bank, off=off:
                    nc.tensor.matmul(
                        bank[0:16, 0:512],
                        w_bcd_res[:, k * 48 + off: k * 48 + off + 16],
                        xc[:, k * TT:(k + 1) * TT],
                        start=(k == 0), stop=(k == NKE - 1))),
                    waits=(wl if k == 0 else []), inc=("pe" if k == NKE - 1 else None))
            bcd_done[bn] = pe_v

        em.emit("vector", lambda: nc.vector.tensor_scalar(
            out=u16[:, :], in0=P2[0:16, 0:512], scalar1=bcd_b3[:, 0:1],
            scalar2=None, op0=ALU.add),
            waits=[("pe", bcd_done["P2"])], drain=True)
        em.emit("vector", lambda: nc.vector.tensor_scalar(
            out=du16[:, :], in0=P4[0:16, 0:512], scalar1=bcd_b3[:, 2:3],
            scalar2=None, op0=ALU.add),
            waits=[("pe", bcd_done["P4"])] + ([G["cs"]] if G["cs"] is not None else []),
            drain=True)
        em.emit("vector", lambda: nc.vector.tensor_tensor(
            u16[:, :], u16[:, :], du16[:, :], ALU.mult), drain=True)
        dve_c16 = em.emit("vector", lambda: nc.vector.tensor_scalar(
            out=c16[:, :], in0=P3[0:16, 0:512], scalar1=bcd_b3[:, 1:2],
            scalar2=None, op0=ALU.add),
            waits=[("pe", bcd_done["P3"])], drain=True, inc="dve")
        bank_set("P2", "dve", dve_c16)
        bank_set("P3", "dve", dve_c16)
        bank_set("P4", "dve", dve_c16)

        if first:
            em.emit("vector", lambda: nc.vector.tensor_tensor_scan(
                st16s[:, :], decay16[:, :], u16[:, :], 0.0,
                ALU.mult, ALU.add), drain=True)
        else:
            em.emit("vector", lambda: nc.vector.tensor_tensor_scan(
                st16s[:, :], decay16[:, :], u16[:, :],
                state_col[:, 0:1], ALU.mult, ALU.add), drain=True)
        em.emit("vector", lambda: nc.vector.tensor_copy(
            state_col[:, 0:1], st16s[:, TT - 1:TT]), drain=True)
        cs_ready = em.emit("vector", lambda: nc.vector.tensor_tensor(
            cs16[:, :], c16[:, :], st16s[:, :], ALU.mult),
            drain=True, inc="dve")
        st["cs_ready"] = cs_ready

    def emit_ys(i):
        st = TS_[i]
        cs_ready = st["cs_ready"]
        ys_ready = {}
        last_cs_pe = None
        for jj in range(NJ):
            b = jj % 2
            bank, bname = PMAIN[b], ("P0" if b == 0 else "P1")
            wl = [("dve", cs_ready)] + bank_wait(bname)
            pe_v = em.emit("tensor", (lambda jj=jj, bank=bank:
                nc.tensor.transpose(
                    bank[:, 0:16], cs16[0:16, jj * 128:(jj + 1) * 128],
                    ident_f32[0:16, 0:16])),
                waits=wl, inc="pe")
            dve_v = em.emit("vector", (lambda jj=jj, bank=bank:
                nc.vector.tensor_reduce(
                    out=ys_tm[:, jj:jj + 1], in_=bank[:, 0:16],
                    axis=mybir.AxisListType.X, op=ALU.add)),
                waits=[("pe", pe_v)], drain=True, inc="dve")
            ys_ready[jj] = dve_v
            bank_set(bname, "dve", dve_v)
            last_cs_pe = pe_v
        G["cs"] = ("pe", last_cs_pe)
        st["ys_ready"] = ys_ready

    def emit_out(i):
        st = TS_[i]
        base, first, last = tile_base(i)
        ys_ready = st["ys_ready"]
        evac_out = {}
        pe_last = {}
        PBANKS4 = [("P0", P0), ("P1", P1), ("P2", P2), ("P3", P3)]
        for n2 in range(2):
            for h in range(2):
                ob_i = (n2 * 2 + h) % 2
                wl = [G["outslab"][ob_i]] if G["outslab"][ob_i] is not None else []
                if first_use["out"]:
                    wl.extend(toks_out)
                    first_use["out"] = False
                def mk_sl(n2=n2, h=h, ob_i=ob_i):
                    srcp = bass.AP(sc_out.tensor, n2 * 128 * 8192 + h * 4096,
                                   [[8192, 128], [1, 4096]])
                    return nc.scalar.dma_start(out=w_out_sl[ob_i][:, :], in_=srcp)
                sl_v = em.dma("scalar", mk_sl, stream="wo", waits=wl)
                for jj in range(NJ):
                    bname, bank = PBANKS4[jj]
                    wl = [sl_v]
                    if h == 0:
                        wl += bank_wait(bname)
                    for k8 in range(8):
                        k = h * 8 + k8
                        v = em.emit("tensor", (lambda k=k, k8=k8, jj=jj, bank=bank, h=h, ob_i=ob_i:
                            nc.tensor.matmul(
                                bank[:, 0:512],
                                g_sb[:, k * TT + jj * 128: k * TT + (jj + 1) * 128],
                                w_out_sl[ob_i][:, k8 * 512:(k8 + 1) * 512],
                                start=(k == 0), stop=(h == 1 and k8 == 7))),
                            waits=(wl if k8 == 0 else []),
                            inc=("pe" if k8 == 7 else None))
                        if k8 == 7:
                            pe_last[jj] = v
                G["outslab"][ob_i] = ("pe", em.c("pe"))
                if h == 1:
                    for jj in range(NJ):
                        bname, bank = PBANKS4[jj]
                        awl = [("pe", pe_last[jj])]
                        if n2 == 0 and jj == 0 and G["r2f"] is not None:
                            awl.append(G["r2f"])
                        dve_v = em.emit("vector", (lambda bank=bank, jj=jj, n2=n2:
                            nc.vector.scalar_tensor_tensor(
                                out=r2f[:, jj * D + n2 * 512: jj * D + n2 * 512 + 512],
                                in0=bank[:, 0:512],
                                scalar=ys_tm[:, jj:jj + 1],
                                in1=bass.AP(xres, jj * D + n2 * 512,
                                            [[4 * D, 128], [1, 512]]),
                                op0=ALU.mult, op1=ALU.add)),
                            waits=awl, drain=True, inc="dve")
                        bank_set(bname, "dve", dve_v)
                        evac_out[(n2, jj)] = dve_v
        G["g"] = ("pe", em.c("pe"))
        G["xc"] = ("pe", em.c("pe"))

        res_done = {}
        for jj in range(NJ):
            res_done[jj] = evac_out[(1, jj)]
            G["xres"][jj] = ("dve", evac_out[(1, jj)])
        st["res_done"] = res_done

    def prep_xres(i, jj):
        base, first, last = tile_base(i)
        wl = [G["xres"][jj]] if G["xres"][jj] is not None else []
        def mk_xr(base=base, jj=jj):
            srcp = bass.AP(x_h.tensor, (base + jj * 128) * D, [[D, 128], [1, D]])
            dst = bass.AP(xres, jj * D, [[4 * D, 128], [1, D]])
            return nc.sync.dma_start(out=dst, in_=srcp)
        xr_tok = em.dma("sync", mk_xr, stream="xr", waits=wl)
        wl2 = [xr_tok]
        if i == 0 and jj == 0:
            wl2.append(prep_outb_done)
        em.emit("vector", (lambda jj=jj: nc.vector.scalar_tensor_tensor(
            out=bass.AP(xres, jj * D, [[4 * D, 128], [1, D]]),
            in0=bass.AP(xres, jj * D, [[4 * D, 128], [1, D]]),
            scalar=2.0, in1=outb_bc[:, :],
            op0=ALU.mult, op1=ALU.add)),
            waits=wl2, drain=True, inc="dve")

    def emit_ln(src_fn, waits):
        em.emit("vector", lambda src_fn=src_fn: nc.vector.bn_stats(
            stat[:, 0:6], src_fn()[:, 0:512]), waits=waits, drain=True)
        em.emit("vector", lambda src_fn=src_fn: nc.vector.bn_stats(
            stat[:, 6:12], src_fn()[:, 512:1024]), drain=True)
        em.emit("vector", lambda: nc.vector.bn_aggr(
            stat[:, 12:14], stat[:, 0:12]), drain=True)
        em.emit("vector", lambda: nc.vector.tensor_scalar(
            out=stat[:, 14:15], in0=stat[:, 12:13], scalar1=-1.0,
            scalar2=None, op0=ALU.mult), drain=True)
        em.emit("vector", lambda: nc.vector.tensor_scalar(
            out=stat[:, 2:3], in0=stat[:, 13:14], scalar1=1e-5, scalar2=None,
            op0=ALU.add), drain=True)
        dve_v = em.emit("vector", lambda: nc.vector.reciprocal(
            stat[:, 3:4], stat[:, 2:3]), drain=True, inc="dve")
        act_v = em.emit("scalar", lambda: nc.scalar.activation(
            out=stat[:, 6:7], in_=stat[:, 3:4], func=AF.Sqrt),
            waits=[("dve", dve_v)], drain=True, inc="act")
        return act_v

    def emit_tail_preA(i, jj):
        st = TS_[i]
        res_done = st["res_done"]
        wl = [("dve", res_done[jj])]
        if G["ln1"] is not None and jj == 0:
            wl.append(G["ln1"])
        ln_act = emit_ln(lambda jj=jj: r2f[:, jj * D:(jj + 1) * D], wl)
        ln1_done = em.emit("vector", (lambda jj=jj: nc.vector.tensor_scalar(
            out=ln1[:, jj * D:(jj + 1) * D], in0=r2f[:, jj * D:(jj + 1) * D],
            scalar1=stat[:, 14:15], scalar2=stat[:, 6:7],
            op0=ALU.add, op1=ALU.mult)),
            waits=[("act", ln_act)], drain=True, inc="dve")
        st.setdefault("ln1_done", {})[jj] = ln1_done
        if jj == NJ - 1:
            G["r2f"] = ("dve", ln1_done)

    def emit_tail_preB(i, jj):
        st = TS_[i]
        base, first, last = tile_base(i)
        ln1_done = st["ln1_done"][jj]
        if True:
            for half in range(2):
                bank, bname = (PT, "PT") if half == 0 else (PS1, "PS1")
                wl0 = [("dve", ln1_done)] + bank_wait(bname)
                pe_v = None
                for c4 in range(4):
                    c = half * 4 + c4
                    pe_v = em.emit("tensor", (lambda jj=jj, c=c, c4=c4, bank=bank:
                        nc.tensor.transpose(
                            bank[:, c4 * 128:(c4 + 1) * 128],
                            ln1[:, jj * D + c * 128: jj * D + (c + 1) * 128],
                            ident_f32[:, :])),
                        waits=(wl0 if c4 == 0 else []),
                        inc=("pe" if c4 == 3 else None))
                dve_v = em.emit("vector", (lambda half=half, bank=bank:
                    nc.vector.tensor_scalar(
                        out=qA[:, half * 512:(half + 1) * 512],
                        in0=bank[:, 0:512], scalar1=r_bc[:, 0:1], scalar2=448.0,
                        op0=ALU.mult, op1=ALU.min)),
                    waits=[("pe", pe_v)], drain=True, inc="dve")
                bank_set(bname, "dve", dve_v)

            qwl = []
            if G["xq"][jj % 2] is not None:
                qwl.append(G["xq"][jj % 2])
            xq_ready = em.emit("vector", (lambda jj=jj: nc.vector.tensor_scalar(
                out=xqs[jj % 2], in0=qA, scalar1=-448.0, scalar2=None,
                op0=ALU.max)), waits=qwl, drain=True, inc="dve")
            st.setdefault("xq_ready", {})[jj] = xq_ready

    def emit_tail_ffn(i, jj):
        st = TS_[i]
        base, first, last = tile_base(i)
        xq_ready = st["xq_ready"][jj]
        if True:
            ffn_evac = {}
            for n2 in range(2):
                bank, bname = (P4, "P4") if n2 == 0 else (P5, "P5")
                wl = [("dve", xq_ready)] + bank_wait(bname)
                if first_use["ffn"]:
                    wl.extend(toks_ffn)
                    first_use["ffn"] = False
                for k in range(NKD):
                    em.emit("tensor", (lambda k=k, n2=n2, bank=bank, jj=jj:
                        nc.tensor.matmul(
                            bank[:, 0:512],
                            xqs[jj % 2][:, k * 128:(k + 1) * 128],
                            w_ffn_res[:, k * 1024 + n2 * 512: k * 1024 + n2 * 512 + 512],
                            start=(k == 0), stop=False)),
                        waits=(wl if k == 0 else []))
                pe_v = em.emit("tensor", (lambda n2=n2, bank=bank:
                    nc.tensor.matmul(
                        bank[:, 0:512], ones_row_bf[0:1, :],
                        bp_row_bf[0:1, n2 * 512:(n2 + 1) * 512],
                        start=False, stop=True)),
                    inc="pe")
                ob = outb[jj % 2]
                awl = [("pe", pe_v)]
                if G["outb"][jj % 2] is not None and n2 == 0:
                    awl.append(G["outb"][jj % 2])
                act_v = em.emit("scalar", (lambda n2=n2, bank=bank, ob=ob:
                    nc.scalar.activation(
                        out=ob[:, n2 * 512:(n2 + 1) * 512], in_=bank[:, 0:512],
                        func=AF.Identity, scale=sp_bc[:, 0:1])),
                    waits=awl, drain=True, inc="act")
                bank_set(bname, "act", act_v)
                ffn_evac[n2] = act_v
            G["xq"][jj % 2] = ("pe", em.c("pe"))

            ob = outb[jj % 2]
            r4_done = em.emit("vector", (lambda jj=jj, ob=ob:
                nc.vector.tensor_tensor(
                    ob[:, :], ob[:, :], ln1[:, jj * D:(jj + 1) * D], ALU.add)),
                waits=[("act", max(ffn_evac[0], ffn_evac[1]))],
                drain=True, inc="dve")
            ln_act = emit_ln(lambda ob=ob: ob[:, :], [("dve", r4_done)])
            out_done = em.emit("vector", (lambda ob=ob: nc.vector.tensor_scalar(
                out=ob[:, :], in0=ob[:, :],
                scalar1=stat[:, 14:15], scalar2=stat[:, 6:7],
                op0=ALU.add, op1=ALU.mult)),
                waits=[("act", ln_act)], drain=True, inc="dve")

            def mk_out(base=base, jj=jj, ob=ob):
                dst = bass.AP(y_h.tensor, (base + jj * 128) * D, [[D, 128], [1, D]])
                return nc.gpsimd.dma_start(out=dst, in_=ob[:, :])
            st_v = em.dma("gpsimd", mk_out, stream="st", waits=[("dve", out_done)])
            G["outb"][jj % 2] = st_v
            G["ln1"] = ("dve", out_done)

    import os as _os
    _limit = int(_os.environ.get("MK_LIMIT_TILES", "0")) or (NSEQ * TPS)
    emit_front(0)
    for g0 in range(6):
        issue_gate_pair(0, g0)
    for p in range(NJ):
        emit_gate_part(0, p)
        emit_conv_part(0, p)
    for i in range(_limit):
        emit_mixer(i)
        if i + 1 < _limit:
            emit_front(i + 1)
        emit_ys(i)
        emit_out(i)
        if i + 1 < _limit:
            for g0 in range(6):
                issue_gate_pair(i + 1, g0)
        for jj in range(NJ):
            emit_tail_preA(i, jj)
        for jj in range(NJ):
            if i + 1 < _limit:
                emit_gate_part(i + 1, jj)
            emit_tail_preB(i, jj)
            if i + 1 < _limit:
                emit_conv_part(i + 1, jj)
            if jj > 0:
                emit_tail_ffn(i, jj - 1)
        emit_tail_ffn(i, NJ - 1)

    em.emit("gpsimd", lambda: None, waits=em.stream_total("st"))

    sem_stack = contextlib.ExitStack()
    sems = {name: sem_stack.enter_context(nc.semaphore(name)) for name in em.counts}

    with nc.Block() as block:
        def run_ops(eng_obj, eng_name):
            for fn, waits, inc, inc_n, drain in em.ops[eng_name]:
                if drain:
                    eng_obj.drain()
                for (sname, val) in waits:
                    if val:
                        eng_obj.wait_ge(sems[sname], val)
                inst = fn()
                if inc is not None:
                    if inst is None:
                        inst = eng_obj.drain()
                    inst.then_inc(sems[inc], inc_n)

        @block.sync
        def _(sync):
            run_ops(sync, "sync")

        @block.tensor
        def _(tensor):
            run_ops(tensor, "tensor")

        @block.scalar
        def _(scalar):
            run_ops(scalar, "scalar")

        @block.vector
        def _(vector):
            run_ops(vector, "vector")

        @block.gpsimd
        def _(gpsimd):
            run_ops(gpsimd, "gpsimd")

    es.close()
    sem_stack.close()
    return nc


_NC_CACHE = None


def kernel(**inputs):
    global _NC_CACHE
    import ml_dtypes

    f32 = np.float32
    cc = np.ascontiguousarray
    x = cc(np.asarray(inputs["L0_attn_x"], dtype=f32))
    in_w = np.asarray(inputs["L0_attn_in_w"], dtype=f32)
    in_b = np.asarray(inputs["L0_attn_in_b"], dtype=f32)
    conv_w = np.asarray(inputs["L0_attn_conv_w"], dtype=f32)
    conv_b = np.asarray(inputs["L0_attn_conv_b"], dtype=f32)
    dt_w = np.asarray(inputs["L0_attn_dt_w"], dtype=f32)
    dt_b = np.asarray(inputs["L0_attn_dt_b"], dtype=f32)
    B_w = np.asarray(inputs["L0_attn_B_w"], dtype=f32)
    B_b = np.asarray(inputs["L0_attn_B_b"], dtype=f32)
    C_w = np.asarray(inputs["L0_attn_C_w"], dtype=f32)
    C_b = np.asarray(inputs["L0_attn_C_b"], dtype=f32)
    D_w = np.asarray(inputs["L0_attn_D_w"], dtype=f32)
    D_b = np.asarray(inputs["L0_attn_D_b"], dtype=f32)
    out_w = np.asarray(inputs["L0_attn_out_w"], dtype=f32)
    out_b = np.asarray(inputs["L0_attn_out_b"], dtype=f32)
    ffn_w = np.asarray(inputs["L0_ffn_weight_fp8"], dtype=f32)
    ffn_b = np.asarray(inputs["L0_ffn_bias"], dtype=f32)
    in_scale = np.asarray(inputs["L0_ffn_input_scale"], dtype=f32)
    w_scale = np.asarray(inputs["L0_ffn_weight_scale"], dtype=f32)

    common = {
        "in_wT": cc(in_w.T),
        "dt_wT": cc(dt_w.T),
        "out_wT": cc(out_w.T),
        "ffn_wT": cc(ffn_w.T),
        "bcd_wT": cc(np.concatenate([B_w.T, C_w.T, D_w.T], axis=1)),
        "in_b_pm": cc(in_b.reshape(32, 128).T),
        "dt_b_pm": cc(dt_b.reshape(16, 128).T),
        "conv_b_pm": cc(conv_b.reshape(16, 128).T),
        "conv_w_pm": cc(conv_w[:, 0, :].reshape(16, 128, 3).transpose(1, 0, 2).reshape(128, 48)),
        "bcd_b3": cc(np.stack([B_b, C_b, D_b], axis=1)),
        "out_b_row": cc(out_b[None, :]),
        "ffn_b_row": cc(ffn_b[None, :]),
        "in_scale": cc(in_scale.reshape(1, 1)),
        "w_scale": cc(w_scale.reshape(1, 1)),
        "ident_f32": np.eye(128, dtype=f32),
        "ident_bf": np.eye(128, dtype=ml_dtypes.bfloat16),
        "ones_row_f32": np.ones((1, 128), dtype=f32),
        "ones_row_bf": np.ones((1, 128), dtype=ml_dtypes.bfloat16),
        "ones_col_bf": np.ones((128, 1), dtype=ml_dtypes.bfloat16),
    }
    n_cores = 8
    xr = x.reshape(32, TSEQ, D)
    in_maps = []
    for c in range(n_cores):
        m = dict(common)
        m["x"] = cc(xr[c * NSEQ:(c + 1) * NSEQ].reshape(NTOK, D))
        in_maps.append(m)

    if _NC_CACHE is None:
        _NC_CACHE = build_nc()
    res = run_bass_kernel_spmd(_NC_CACHE, in_maps, core_ids=list(range(n_cores)))
    out = np.concatenate(
        [res.results[c]["y"].reshape(NSEQ, TSEQ, D) for c in range(n_cores)], axis=0)
    return np.ascontiguousarray(out, dtype=np.float32)
